# version 22 (final)
# baseline (speedup 1.0000x reference)
import contextlib
import numpy as np
import ml_dtypes
import concourse.bass as bass
import concourse.mybir as mybir
from concourse.bass_utils import run_bass_kernel_spmd

F32 = mybir.dt.float32
BF16 = mybir.dt.bfloat16
AF = mybir.ActivationFunctionType
ALU = mybir.AluOpType
AX = mybir.AxisListType
BF = ml_dtypes.bfloat16

B, S, D, H = 4, 4096, 1024, 16
NCORE = 8
RB = 2048
NP, TP = 4, 512
NT = TP // 128
NKEY = 320
MASKV = -50.0
EPS = 1e-5
ES = (("hi", "hi"), ("hi", "lo"), ("lo", "hi"))
ENGS = ("sync", "gpsimd", "tensor", "vector", "scalar")
LIMIT = None
SKIP_TT = False


class _Stop(Exception):
    pass


def _merge(d, upd):
    for k, v in upd.items():
        if v and v > d.get(k, 0):
            d[k] = v


def build_program():
    nc = bass.Bass("TRN2", target_bir_lowering=False, detect_race_conditions=False)

    x_h = nc.dram_tensor("x", [RB, D], F32, kind="ExternalInput").ap()
    xpre_h = nc.dram_tensor("xpre", [192, D], F32, kind="ExternalInput").ap()
    wqkv_h = nc.dram_tensor("wqkv", [D, 3 * D], F32, kind="ExternalInput").ap()
    wout_h = nc.dram_tensor("wout", [D, D], F32, kind="ExternalInput").ap()
    ttu_h = nc.dram_tensor("ttu", [8, 512, 512], F32, kind="ExternalInput").ap()
    ttv_h = nc.dram_tensor("ttv", [8, 512, 512], F32, kind="ExternalInput").ap()
    biasb_h = nc.dram_tensor("biasb", [128, D], F32, kind="ExternalInput").ap()
    masks_h = nc.dram_tensor("masks", [16, 128, NKEY], BF16, kind="ExternalInput").ap()
    idf_h = nc.dram_tensor("idf", [128, 128], F32, kind="ExternalInput").ap()
    idb_h = nc.dram_tensor("idb", [128, 128], BF16, kind="ExternalInput").ap()
    out_h = nc.dram_tensor("out", [RB, D], F32, kind="ExternalOutput").ap()

    with contextlib.ExitStack() as ctx:
        A = ctx.enter_context

        idf = A(nc.sbuf_tensor("s_idf", [128, 128], F32))
        idb = A(nc.sbuf_tensor("s_idb", [128, 128], BF16))
        biasb = A(nc.sbuf_tensor("s_biasb", [128, D], F32))
        wout_hi = A(nc.sbuf_tensor("s_wout_hi", [128, 8 * 1024], BF16))
        wout_lo = A(nc.sbuf_tensor("s_wout_lo", [128, 8 * 1024], BF16))
        wffn_hi = A(nc.sbuf_tensor("s_wffn_hi", [128, 8 * 1024], BF16))
        wffn_lo = A(nc.sbuf_tensor("s_wffn_lo", [128, 8 * 1024], BF16))
        wbuf = [A(nc.sbuf_tensor(f"s_wbuf{i}", [128, 512], F32)) for i in range(3)]
        whi = [A(nc.sbuf_tensor(f"s_whi{i}", [128, 512], BF16)) for i in range(3)]
        wlo = [A(nc.sbuf_tensor(f"s_wlo{i}", [128, 512], BF16)) for i in range(3)]
        gk_f = A(nc.sbuf_tensor("s_gk_f", [128, 8 * 64], F32))
        gv_f = A(nc.sbuf_tensor("s_gv_f", [64, 1024], F32))
        x_tm = [A(nc.sbuf_tensor(f"s_xtm{i}", [128, D], F32)) for i in range(2)]
        xf_hi = A(nc.sbuf_tensor("s_xf_hi", [128, 8 * 640], BF16))
        xf_lo = A(nc.sbuf_tensor("s_xf_lo", [128, 8 * 640], BF16))
        xp_hi = A(nc.sbuf_tensor("s_xp_hi", [128, 8 * 64], BF16))
        xp_lo = A(nc.sbuf_tensor("s_xp_lo", [128, 8 * 64], BF16))
        q_f = A(nc.sbuf_tensor("s_q_f", [128, 8 * 512], F32))
        k_f = A(nc.sbuf_tensor("s_k_f", [128, 8 * 640], F32))
        v_f = A(nc.sbuf_tensor("s_v_f", [128, 5 * 1024], F32))
        mask_sb = [A(nc.sbuf_tensor(f"s_mask{i}", [128, NKEY], BF16)) for i in range(2)]
        e_sb = [A(nc.sbuf_tensor(f"s_e{i}", [128, NKEY], F32)) for i in range(2)]
        p_f = [A(nc.sbuf_tensor(f"s_p_f{i}", [128, NKEY], F32)) for i in range(2)]
        pt_f = [A(nc.sbuf_tensor(f"s_pt_f{i}", [128, 384], F32)) for i in range(2)]
        attn_hi = A(nc.sbuf_tensor("s_attn_hi", [128, 8 * 128], BF16))
        attn_lo = A(nc.sbuf_tensor("s_attn_lo", [128, 8 * 128], BF16))
        negm = A(nc.sbuf_tensor("s_negm", [128, 16], F32))
        sumv = A(nc.sbuf_tensor("s_sumv", [128, 16], F32))
        recv = A(nc.sbuf_tensor("s_recv", [128, 16], F32))
        xres = A(nc.sbuf_tensor("s_xres", [128, D], F32))
        x1pre = A(nc.sbuf_tensor("s_x1pre", [128, D], F32))
        x1_tm = A(nc.sbuf_tensor("s_x1tm", [128, D], F32))
        x1f_hi = A(nc.sbuf_tensor("s_x1f_hi", [128, 8 * 128], BF16))
        x1f_lo = A(nc.sbuf_tensor("s_x1f_lo", [128, 8 * 128], BF16))
        out_sb = A(nc.sbuf_tensor("s_out0", [128, D], F32))
        ln_st = [A(nc.sbuf_tensor("s_lnst0", [128, 12], F32))] * 2
        ln_sqm = [A(nc.sbuf_tensor("s_lnsqm0", [128, 4], F32))] * 2
        ln_r = [A(nc.sbuf_tensor("s_lnr0", [128, 8], F32))] * 2
        ln_sq = [A(nc.sbuf_tensor("s_lnsq0", [128, 1], F32))] * 2

        PS = [A(nc.psum_tensor(f"ps{i}", [128, 512], F32)) for i in range(8)]

        semh = {n: A(nc.semaphore(f"sem_{n}"))
                for n in ("ld", "st", "tv", "vv", "sv", "lc", "lt",
                          "lw0", "lw1", "lw2", "lx0", "lx1", "lr", "lm0", "lm1")}
        block = A(nc.Block())

        tta = out_sb[:, 0:512]
        ttt = out_sb[:, 512:1024]
        xf_f32 = xf_hi.bitcast(F32)

        Q = {e: [] for e in ENGS}
        cnt = {n: 0 for n in semh}
        nemit = [0]

        def E(eng, make, inc=None, n=1):
            nemit[0] += 1
            off = LIMIT is not None and nemit[0] > LIMIT
            if inc is None:
                if not off:
                    Q[eng].append(make)
                return None
            cnt[inc] += n
            h = semh[inc]
            if not off:
                Q[eng].append(lambda e, mk=make, h=h, n=n: mk(e).then_inc(h, n))
            return cnt[inc]

        def W(eng, waits):
            if LIMIT is not None and nemit[0] >= LIMIT:
                return
            for s, v in waits.items():
                if v:
                    Q[eng].append(lambda e, h=semh[s], v=v: e.wait_ge(h, v))

        bankfree = {i: {} for i in range(8)}

        def mm_group(bank, mms, extra_waits=()):
            W("tensor", bankfree[bank])
            for w in extra_waits:
                W("tensor", w)
            tvv = None
            for i, mk in enumerate(mms):
                first, last = i == 0, i == len(mms) - 1
                tvv = E("tensor", (lambda e, mk=mk, f=first, l=last: mk(e, f, l)),
                        "tv" if last else None)
            return tvv

        E("sync", lambda e: e.dma_start(out=idf[:], in_=idf_h[:]), "lc", 16)
        E("sync", lambda e: e.dma_start(out=idb[:], in_=idb_h[:]), "lc", 16)
        E("sync", lambda e: e.dma_start(out=biasb[:], in_=biasb_h[:]), "lc", 16)
        consts_ready = {"lc": 48}

        ffn_ready = {}
        wout_ready = {}
        _prep = {}

        def build_weight_prep_fillers(xfree):
            fillers = []
            st = {"tt_mult_done": {0: dict(xfree), 1: dict(xfree)}, "it": 0,
                  "accv": None}

            def mk_ttc(qq, ri, c):
                def go():
                    idx = ri * 4 + c
                    par = st["it"] % 2
                    st["it"] += 1
                    ttu = xf_f32[:, 1024 * par:1024 * par + 512]
                    ttv = xf_f32[:, 1024 * par + 512:1024 * par + 1024]
                    W("sync", st["tt_mult_done"][par])
                    E("sync", lambda e, idx=idx, qq=qq, ttu=ttu: e.dma_start(
                        out=ttu, in_=ttu_h[idx, 128 * qq:128 * (qq + 1), :]), "lt", 16)
                    l2 = E("sync", lambda e, idx=idx, qq=qq, ttv=ttv: e.dma_start(
                        out=ttv, in_=ttv_h[idx, 128 * qq:128 * (qq + 1), :]), "lt", 16)
                    W("vector", {"lt": l2})
                    if c == 0:
                        W("vector", st.get("plc_sv", {}))
                        st["accv"] = E("vector", lambda e, ttu=ttu, ttv=ttv: e.tensor_tensor(
                            out=tta, in0=ttu, in1=ttv, op=ALU.mult), "vv")
                    else:
                        E("vector", lambda e, ttu=ttu, ttv=ttv: e.tensor_tensor(
                            out=ttt, in0=ttu, in1=ttv, op=ALU.mult), "vv")
                        st["accv"] = E("vector", lambda e: e.tensor_tensor(
                            out=tta, in0=tta, in1=ttt, op=ALU.add), "vv")
                    st["tt_mult_done"][par] = {"vv": st["accv"]}
                return go

            def mk_plc(sl, coff, sgn):
                def go():
                    dh = wffn_hi[:, 1024 * sl + coff:1024 * sl + coff + 512]
                    dl = wffn_lo[:, 1024 * sl + coff:1024 * sl + coff + 512]
                    W("scalar", {"vv": st["accv"]})
                    lsv = E("scalar", lambda e, dh=dh, sgn=sgn: e.activation(
                        out=dh, in_=tta, func=AF.Identity, bias=0.0, scale=sgn), "sv")
                    W("vector", {"sv": lsv})
                    lvv = E("vector", lambda e, dh=dh, dl=dl, sgn=sgn:
                            e.scalar_tensor_tensor(out=dl, in0=tta, scalar=sgn, in1=dh,
                                                   op0=ALU.mult, op1=ALU.subtract), "vv")
                    st["plc_sv"] = {"sv": lsv}
                    _merge(ffn_ready, {"sv": lsv, "vv": lvv})
                return go

            for qq in range(4 if not SKIP_TT else 0):
                for ri in range(2):
                    for c in range(4):
                        fillers.append(mk_ttc(qq, ri, c))
                    plc = ([(qq, 0, 1.0), (4 + qq, 512, 1.0)] if ri == 0
                           else [(qq, 512, 1.0), (4 + qq, 0, -1.0)])
                    for sl, coff, sgn in plc:
                        fillers.append(mk_plc(sl, coff, sgn))
            return fillers

        def build_wout_fillers():
            def mk(pc):
                def go():
                    fc, g = pc // 2, pc % 2
                    db = wpiece[0] % 3
                    wpiece[0] += 1
                    W("sync", wdb_free[db])
                    ldv = E("sync", lambda e, fc=fc, g=g, db=db: e.dma_start(
                        out=wbuf[db][:],
                        in_=wout_h[128 * fc:128 * (fc + 1),
                                   512 * g:512 * g + 512]), f"lw{db}", 16)
                    W("sync", wuse_free[db])
                    c0 = 1024 * fc + 512 * g
                    W("scalar", {f"lw{db}": ldv})
                    W("scalar", wuse_free[db])
                    sv = E("scalar", lambda e, c0=c0, db=db: e.copy(
                        out=wout_hi[:, c0:c0 + 512], in_=wbuf[db][:]), "sv")
                    W("vector", {f"lw{db}": ldv, "sv": sv})
                    vv = E("vector", lambda e, c0=c0, db=db: e.tensor_tensor(
                        out=wout_lo[:, c0:c0 + 512], in0=wbuf[db][:],
                        in1=wout_hi[:, c0:c0 + 512], op=ALU.subtract), "vv")
                    wdb_free[db] = {"sv": sv, "vv": vv}
                    _merge(wout_ready, {"sv": sv, "vv": vv})
                return go
            return [mk(pc) for pc in range(16)]

        wdb_free = {0: {}, 1: {}, 2: {}}
        wuse_free = {0: {}, 1: {}, 2: {}}
        xpf_ready = {}
        xpf_free = {}
        gkv_ready = {}
        kvbuf_free = {}
        xfm_free = {}
        mask_free = [{}, {}]
        xres_free = {}
        xtm_free = [{}, {}]
        ptbuf_free = [{}, {}]
        out_free = {}
        ehb_free = [{}, {}]
        phl_free = [{}, {}]
        attnfree_chunk = [{} for _ in range(8)]
        x1fbuf_free = {}
        _lnb = {}
        ln_busy = [_lnb, _lnb]
        wpiece = [0]
        fillers = []
        stcnt_box = [0]
        pending_tail = [None]
        pending_t6 = [None]
        passend = {"attn_tv": 0}

        def emit_ln(src_, dst_, si):
            W("vector", {"vv": cnt["vv"]})
            W("vector", ln_busy[si])
            E("vector", lambda e, si=si, src_=src_: e.bn_stats(
                out=ln_st[si][:, 0:6], in_=src_[:, 0:512]), "vv")
            v2 = E("vector", lambda e, si=si, src_=src_: e.bn_stats(
                out=ln_st[si][:, 6:12], in_=src_[:, 512:1024]), "vv")
            W("scalar", {"vv": v2})
            E("scalar", lambda e, si=si: e.activation(
                out=ln_sqm[si][:], in_=ln_st[si][:, 1:12:3], func=AF.Identity,
                bias=0.0, scale=1.0, accum_out=ln_r[si][:, 0:1]), "sv")
            E("scalar", lambda e, si=si: e.activation(
                out=ln_sqm[si][:], in_=ln_st[si][:, 2:12:3], func=AF.Identity,
                bias=0.0, scale=1.0, accum_out=ln_r[si][:, 1:2]), "sv")
            a3 = E("scalar", lambda e, si=si: e.activation(
                out=ln_sqm[si][:], in_=ln_st[si][:, 1:12:3], func=AF.Square,
                bias=0.0, scale=1.0, accum_out=ln_r[si][:, 2:3]), "sv")
            W("vector", {"sv": a3})
            E("vector", lambda e, si=si: e.tensor_scalar(
                out=ln_r[si][:, 3:4], in0=ln_r[si][:, 0:1], scalar1=0.25,
                scalar2=None, op0=ALU.mult), "vv")
            d2 = E("vector", lambda e, si=si: e.tensor_scalar(
                out=ln_r[si][:, 4:5], in0=ln_r[si][:, 1:2], scalar1=1.0 / 1024.0,
                scalar2=EPS, op0=ALU.mult, op1=ALU.add), "vv")
            W("vector", {"vv": d2})
            E("vector", lambda e, si=si: e.scalar_tensor_tensor(
                out=ln_r[si][:, 5:6], in0=ln_r[si][:, 2:3], scalar=0.25,
                in1=ln_r[si][:, 4:5], op0=ALU.mult, op1=ALU.add), "vv")
            d4 = E("vector", lambda e, si=si: e.tensor_tensor(
                out=ln_r[si][:, 6:7], in0=ln_r[si][:, 3:4], in1=ln_r[si][:, 3:4],
                op=ALU.mult), "vv")
            W("vector", {"vv": d4})
            d5 = E("vector", lambda e, si=si: e.tensor_tensor(
                out=ln_r[si][:, 7:8], in0=ln_r[si][:, 5:6], in1=ln_r[si][:, 6:7],
                op=ALU.subtract), "vv")
            W("scalar", {"vv": d5})
            a4 = E("scalar", lambda e, si=si: e.activation(
                out=ln_sq[si][:], in_=ln_r[si][:, 7:8], func=AF.Ln,
                bias=0.0, scale=1.0), "sv")
            W("scalar", {"sv": a4})
            a5 = E("scalar", lambda e, si=si: e.activation(
                out=ln_r[si][:, 4:5], in_=ln_sq[si][:], func=AF.Exp,
                bias=0.0, scale=-0.5), "sv")
            W("vector", {"sv": a5})
            d7 = E("vector", lambda e, si=si, src_=src_, dst_=dst_: e.tensor_scalar(
                out=dst_, in0=src_, scalar1=ln_r[si][:, 3:4],
                scalar2=ln_r[si][:, 4:5], op0=ALU.subtract, op1=ALU.mult), "vv")
            ln_busy[si].clear()
            ln_busy[si].update({"sv": a5})
            return d7

        for p in range(NP):
            if pending_t6[0] is not None:
                for stage in pending_t6[0]:
                    stage()
                pending_t6[0] = None
            if pending_tail[0] is not None:
                for stage in pending_tail[0]:
                    stage()
                pending_tail[0] = None
            pfx_ready = {}
            if p > 0:
                W("vector", kvbuf_free)
                E("vector", lambda e: e.tensor_copy(
                    out=k_f[:].rearrange("p (c n) -> p c n", n=640)[:, :, 0:128],
                    in_=k_f[:].rearrange("p (c n) -> p c n", n=640)[:, :, 512:640]), "vv")
                a4 = E("vector", lambda e: e.tensor_copy(
                    out=v_f[:, 0:1024], in_=v_f[:, 4096:5120]), "vv")
                pfx_ready = {"vv": a4}

            xfm_ready = {}
            jobs = [("own", t) for t in range(4)]
            if p == 0:
                jobs += [("halo", 0), ("glob", 0)]
            for ji, (kind, t) in enumerate(jobs):
                db = ji % 2
                W("sync", xtm_free[db])
                if kind == "own":
                    ldv = E("sync", lambda e, p=p, t=t, db=db: e.dma_start(
                        out=x_tm[db][:],
                        in_=x_h[TP * p + 128 * t:TP * p + 128 * t + 128, :]), f"lx{db}", 16)
                    nrow = 128
                elif kind == "halo":
                    ldv = E("sync", lambda e, db=db: e.dma_start(
                        out=x_tm[db][:], in_=xpre_h[64:192, :]), f"lx{db}", 16)
                    nrow = 128
                else:
                    ldv = E("sync", lambda e, db=db: e.dma_start(
                        out=x_tm[db][0:64, :], in_=xpre_h[0:64, :]), f"lx{db}", 16)
                    nrow = 64
                ldsem = f"lx{db}"
                tvl = None
                for half4 in range(2):
                    bank = 2 + half4
                    W("tensor", bankfree[bank])
                    if half4 == 0:
                        W("tensor", {ldsem: ldv})
                        if p == 0 and ji == 0:
                            W("tensor", consts_ready)
                    for sl in range(4):
                        dk = 4 * half4 + sl
                        def mkt(e, db=db, dk=dk, bank=bank, sl=sl, nrow=nrow):
                            return e.transpose(
                                out=PS[bank][:, 128 * sl:128 * sl + nrow],
                                in_=x_tm[db][0:nrow, 128 * dk:128 * dk + 128],
                                identity=idf[0:nrow, 0:nrow])
                        tvl = E("tensor", mkt, "tv" if sl == 3 else None)
                    W("scalar", {"tv": tvl})
                    W("vector", {"tv": tvl})
                    if kind == "own" and p > 0:
                        W("scalar", xfm_free)
                        W("vector", xfm_free)
                    slots = []
                    for sl in range(4):
                        dk = 4 * half4 + sl
                        if kind == "own":
                            c0 = 640 * dk + 128 + 128 * t
                            dh = xf_hi[:, c0:c0 + 128]
                            dl = xf_lo[:, c0:c0 + 128]
                            ncol = 128
                        elif kind == "halo":
                            dh = xf_hi[:, 640 * dk:640 * dk + 128]
                            dl = xf_lo[:, 640 * dk:640 * dk + 128]
                            ncol = 128
                        else:
                            dh = xp_hi[:, 64 * dk:64 * dk + 64]
                            dl = xp_lo[:, 64 * dk:64 * dk + 64]
                            ncol = 64
                        slots.append((sl, dh, dl, ncol))
                    evs = evv = None
                    for sl, dh, dl, ncol in slots:
                        evs = E("scalar", lambda e, bank=bank, sl=sl, dh=dh, ncol=ncol: e.copy(
                            out=dh, in_=PS[bank][:, 128 * sl:128 * sl + ncol]), "sv")
                    W("vector", {"sv": evs})
                    for sl, dh, dl, ncol in slots:
                        evv = E("vector", lambda e, bank=bank, sl=sl, dh=dh, dl=dl, ncol=ncol:
                                e.tensor_tensor(out=dl,
                                                in0=PS[bank][:, 128 * sl:128 * sl + ncol],
                                                in1=dh, op=ALU.subtract), "vv")
                    bankfree[bank] = {"sv": evs, "vv": evv}
                xtm_free[db] = {"tv": tvl}
                if kind == "own":
                    _merge(xfm_ready, {"sv": evs, "vv": evv})
                else:
                    _merge(xpf_ready, {"sv": evs, "vv": evv})

            def load_wpiece(dma_make):
                db = wpiece[0] % 3
                wpiece[0] += 1
                W("sync", wdb_free[db])
                ldv = E("sync", lambda e, db=db, mk=dma_make: mk(e, db), f"lw{db}", 16)
                W("scalar", {f"lw{db}": ldv})
                W("scalar", wuse_free[db])
                sv = E("scalar", lambda e, db=db: e.copy(out=whi[db][:], in_=wbuf[db][:]), "sv")
                W("vector", {"sv": sv})
                vv = E("vector", lambda e, db=db: e.tensor_tensor(
                    out=wlo[db][:], in0=wbuf[db][:], in1=whi[db][:], op=ALU.subtract), "vv")
                wdb_free[db] = {"sv": sv, "vv": vv}
                return db, {"sv": sv, "vv": vv}

            q_ready, k_ready = {}, {}
            qkv_last_tv = 0
            def load_qk_pair(sec, fc):
                gcol = (0 if sec == "q" else 1024) + 128 * fc
                dbs, wreadys = [], []
                for hh in range(2):
                    db_, wr_ = load_wpiece(
                        lambda e, db, gcol=gcol, hh=hh: e.dma_start(
                            out=wbuf[db][:].rearrange("p (dk c) -> p dk c", c=128),
                            in_=wqkv_h[512 * hh:512 * hh + 512,
                                       gcol:gcol + 128].rearrange(
                                "(dk p) c -> p dk c", p=128)))
                    dbs.append(db_)
                    wreadys.append(wr_)
                return dbs, wreadys

            qk_jobs = [(sec, fc) for sec in ("q", "k") for fc in range(8)]
            loaded = {qk_jobs[0]: load_qk_pair(*qk_jobs[0])}
            for ji, (sec, fc) in enumerate(qk_jobs):
                    dbs, wreadys = loaded.pop((sec, fc))
                    bank = fc % 2
                    extra = wreadys + [xfm_ready]
                    if p > 0:
                        extra.append(kvbuf_free)
                    mms = []
                    for dk in range(8):
                        for a, b in ES:
                            dbx = dbs[dk // 4]
                            wsrc = (whi if a == "hi" else wlo)[dbx][
                                :, 128 * (dk % 4):128 * (dk % 4) + 128]
                            xsrc = (xf_hi if b == "hi" else xf_lo)[
                                :, 640 * dk + 128:640 * dk + 640]
                            mms.append(lambda e, f, l, w=wsrc, x=xsrc, bank=bank:
                                       e.matmul(out=PS[bank][:, 0:512], lhsT=w, rhs=x,
                                                start=f, stop=l))
                    W("tensor", bankfree[bank])
                    for w_ in extra:
                        W("tensor", w_)
                    tvq = tv_mid = None
                    for i, mk in enumerate(mms):
                        first, last = i == 0, i == len(mms) - 1
                        r = E("tensor", (lambda e, mk=mk, f=first, l=last: mk(e, f, l)),
                              "tv" if (last or i == 11) else None)
                        if i == 11:
                            tv_mid = r
                        if last:
                            tvq = r
                    wuse_free[dbs[0]] = {"tv": tv_mid}
                    qkv_last_tv = tvq
                    wuse = {"tv": tvq}
                    tvp = tvg = None
                    if p == 0 and sec == "k":
                        mmsp, mmsg = [], []
                        for a, b in ES:
                            for dk in range(8):
                                dbx = dbs[dk // 4]
                                wsrc = (whi if a == "hi" else wlo)[dbx][
                                    :, 128 * (dk % 4):128 * (dk % 4) + 128]
                                xfh = xf_hi if b == "hi" else xf_lo
                                xph = xp_hi if b == "hi" else xp_lo
                                mmsp.append(lambda e, f, l, w=wsrc,
                                            x=xfh[:, 640 * dk:640 * dk + 128]:
                                            e.matmul(out=PS[4][:, 0:128], lhsT=w, rhs=x,
                                                     start=f, stop=l))
                                mmsg.append(lambda e, f, l, w=wsrc,
                                            x=xph[:, 64 * dk:64 * dk + 64]:
                                            e.matmul(out=PS[5][:, 0:64], lhsT=w, rhs=x,
                                                     start=f, stop=l))
                        tvp = mm_group(4, mmsp, extra_waits=wreadys + [xpf_ready])
                        tvg = mm_group(5, mmsg, extra_waits=wreadys)
                        qkv_last_tv = tvg
                        wuse = {"tv": tvg}
                        wuse_free[dbs[0]] = wuse
                    wuse_free[dbs[1]] = wuse
                    if ji + 1 < len(qk_jobs):
                        loaded[qk_jobs[ji + 1]] = load_qk_pair(*qk_jobs[ji + 1])
                    if sec == "q":
                        W("scalar", {"tv": tvq})
                        if p > 0:
                            W("scalar", kvbuf_free)
                            W("scalar", pfx_ready)
                        sv1 = E("scalar", lambda e, fc=fc, bank=bank: e.activation(
                            out=q_f[:, 512 * fc:512 * (fc + 1)], in_=PS[bank][:, 0:512],
                            func=AF.Identity, bias=0.0, scale=0.125), "sv")
                        bankfree[bank] = {"sv": sv1}
                        _merge(q_ready, {"sv": sv1})
                    else:
                        W("vector", {"tv": tvq})
                        if p > 0:
                            W("vector", kvbuf_free)
                            W("vector", pfx_ready)
                        sv1 = E("vector", lambda e, fc=fc, bank=bank: e.tensor_copy(
                            out=k_f[:, 640 * fc + 128:640 * fc + 640],
                            in_=PS[bank][:, 0:512]), "vv")
                        bankfree[bank] = {"vv": sv1}
                        _merge(k_ready, {"vv": sv1})
                        if p == 0:
                            W("vector", {"tv": tvg})
                            sv2 = E("vector", lambda e, fc=fc: e.tensor_copy(
                                out=k_f[:, 640 * fc:640 * fc + 128],
                                in_=PS[4][:, 0:128]), "vv")
                            bankfree[4] = {"vv": sv2}
                            sv3 = E("vector", lambda e, fc=fc: e.tensor_copy(
                                out=gk_f[:, 64 * fc:64 * (fc + 1)], in_=PS[5][:, 0:64]), "vv")
                            bankfree[5] = {"vv": sv3}
                            _merge(k_ready, {"vv": max(sv2, sv3)})
                            _merge(gkv_ready, {"vv": sv3})

            v_ready = {}
            vjobs0 = [("own", j, j) for j in range(4)]
            if p == 0:
                vjobs0 += [("pre", None, 4), ("glob", None, 5)]
            for vcol in range(2):
                vjobs = (vjobs0 if vcol == 0 else
                         [(k, t, jb + 2) for (k, t, jb) in vjobs0])
                for _, _, jb in vjobs:
                    W("tensor", bankfree[jb])
                group_last = {jb: None for _, _, jb in vjobs}

                def load_v(dk, vcol=vcol):
                    return load_wpiece(
                        lambda e, db, dk=dk, vcol=vcol: e.dma_start(
                            out=wbuf[db][:],
                            in_=wqkv_h[128 * dk:128 * (dk + 1),
                                       2048 + 512 * vcol:2048 + 512 * vcol + 512]))
                nxt = load_v(0)
                for dk in range(8):
                    db, wready = nxt
                    if dk + 1 < 8:
                        nxt = load_v(dk + 1)
                    W("tensor", wready)
                    if dk == 0:
                        W("tensor", xfm_ready)
                        if p == 0:
                            W("tensor", xpf_ready)
                        if p > 0:
                            W("tensor", kvbuf_free)
                    dk_last_tv = None
                    for ai, (a, b) in enumerate(ES):
                        wsrc = (whi if b == "hi" else wlo)[db][:]
                        for kind, t, jb in vjobs:
                            if kind == "own":
                                c0 = 640 * dk + 128 + 128 * t
                                lh = (xf_hi if a == "hi" else xf_lo)[:, c0:c0 + 128]
                                dst = PS[jb][:, 0:512]
                            elif kind == "pre":
                                lh = (xf_hi if a == "hi" else xf_lo)[
                                    :, 640 * dk:640 * dk + 128]
                                dst = PS[jb][:, 0:512]
                            else:
                                lh = (xp_hi if a == "hi" else xp_lo)[
                                    :, 64 * dk:64 * dk + 64]
                                dst = PS[jb][0:64, 0:512]
                            first = (dk == 0 and ai == 0)
                            last = (dk == 7 and ai == len(ES) - 1)
                            is_dk_last = (ai == len(ES) - 1 and (kind, t, jb) == vjobs[-1])
                            r = E("tensor",
                                  lambda e, lh=lh, w=wsrc, dst=dst, f=first, l=last:
                                  e.matmul(out=dst, lhsT=lh, rhs=w, start=f, stop=l),
                                  "tv" if (is_dk_last or last) else None)
                            if r is not None:
                                dk_last_tv = r
                            if last:
                                group_last[jb] = dk_last_tv
                    wuse_free[db] = {"tv": dk_last_tv}
                last_tv = max(v for v in group_last.values())
                qkv_last_tv = max(qkv_last_tv, last_tv)
                for kind, t, jb in vjobs:
                    W("scalar", {"tv": group_last[jb]})
                    if p > 0:
                        W("scalar", kvbuf_free)
                        W("scalar", pfx_ready)
                    if kind == "own":
                        c0 = 1024 * (t + 1) + 512 * vcol
                        dh = v_f[:, c0:c0 + 512]
                        src = PS[jb][:, 0:512]
                    elif kind == "pre":
                        c0 = 512 * vcol
                        dh = v_f[:, c0:c0 + 512]
                        src = PS[jb][:, 0:512]
                    else:
                        c0 = 512 * vcol
                        dh = gv_f[0:64, c0:c0 + 512]
                        src = PS[jb][0:64, 0:512]
                    sv1 = E("scalar", lambda e, dh=dh, src=src: e.copy(out=dh, in_=src), "sv")
                    bankfree[jb] = {"sv": sv1}
                    _merge(v_ready, {"sv": sv1})
                    if kind == "glob":
                        _merge(gkv_ready, {"sv": sv1})
            _merge(v_ready, pfx_ready)
            _merge(k_ready, pfx_ready)
            if p == 0:
                fillers[:] = build_wout_fillers() + \
                    build_weight_prep_fillers({"tv": qkv_last_tv})


            def make_tail(p_, l_, ldx_, attn_rdy_):
                st = {}

                def t01():
                    W("tensor", bankfree[4])
                    W("tensor", bankfree[5])
                    W("tensor", attn_rdy_)
                    W("tensor", wout_ready)
                    lastfc = None
                    for fc in range(8):
                        for ai, (a, b) in enumerate(ES):
                            asrc = attn_hi if a == "hi" else attn_lo
                            wsrc = wout_hi if b == "hi" else wout_lo
                            first = (fc == 0 and ai == 0)
                            lastm = (fc == 7 and ai == 2)
                            E("tensor", lambda e, a_=asrc, w=wsrc, fc=fc, f=first, l=lastm:
                              e.matmul(out=PS[4][:, 0:512],
                                       lhsT=a_[:, 128 * fc:128 * fc + 128],
                                       rhs=w[:, 1024 * fc:1024 * fc + 512],
                                       start=f, stop=l))
                            lastfc = E("tensor",
                                       lambda e, a_=asrc, w=wsrc, fc=fc, f=first, l=lastm:
                                       e.matmul(out=PS[5][:, 0:512],
                                                lhsT=a_[:, 128 * fc:128 * fc + 128],
                                                rhs=w[:, 1024 * fc + 512:1024 * fc + 1024],
                                                start=f, stop=l),
                                       "tv" if ai == 2 else None)
                        attnfree_chunk[fc] = {"tv": lastfc}
                    st["y_tv"] = lastfc

                def t2():
                    W("vector", {"tv": st["y_tv"], "lr": ldx_})
                    yv = None
                    for g in range(2):
                        yv = E("vector", lambda e, g=g: e.tensor_tensor(
                            out=x1pre[:, 512 * g:512 * g + 512], in0=PS[4 + g][:, 0:512],
                            in1=xres[:, 512 * g:512 * g + 512], op=ALU.add), "vv")
                        bankfree[4 + g] = {"vv": yv}
                    xres_free.clear()
                    xres_free.update({"vv": yv})
                    st["ln1_v"] = emit_ln(x1pre[:], x1_tm[:], 0)

                def t3():
                    x1f_ready = {}
                    for half4 in range(2):
                        bank = 4 + half4
                        W("tensor", bankfree[bank])
                        if half4 == 0:
                            W("tensor", {"vv": st["ln1_v"]})
                        tvl = None
                        for sl in range(4):
                            dk = 4 * half4 + sl
                            def mkt(e, dk=dk, bank=bank, sl=sl):
                                return e.transpose(
                                    out=PS[bank][:, 128 * sl:128 * sl + 128],
                                    in_=x1_tm[:, 128 * dk:128 * dk + 128],
                                    identity=idf[:])
                            tvl = E("tensor", mkt, "tv" if sl == 3 else None)
                        W("scalar", {"tv": tvl})
                        W("scalar", x1fbuf_free)
                        evs = evv = None
                        for sl in range(4):
                            dk = 4 * half4 + sl
                            evs = E("scalar", lambda e, bank=bank, sl=sl, dk=dk: e.copy(
                                out=x1f_hi[:, 128 * dk:128 * dk + 128],
                                in_=PS[bank][:, 128 * sl:128 * sl + 128]), "sv")
                        W("vector", {"sv": evs})
                        W("vector", x1fbuf_free)
                        for sl in range(4):
                            dk = 4 * half4 + sl
                            evv = E("vector", lambda e, bank=bank, sl=sl, dk=dk:
                                    e.tensor_tensor(
                                        out=x1f_lo[:, 128 * dk:128 * dk + 128],
                                        in0=PS[bank][:, 128 * sl:128 * sl + 128],
                                        in1=x1f_hi[:, 128 * dk:128 * dk + 128],
                                        op=ALU.subtract), "vv")
                        bankfree[bank] = {"sv": evs, "vv": evv}
                        _merge(x1f_ready, {"sv": evs, "vv": evv})
                    st["x1f_ready"] = x1f_ready

                def mk_ffn(g):
                    def t45():
                        mms = []
                        for a, b in ES:
                            asrc = x1f_hi if a == "hi" else x1f_lo
                            wsrc = wffn_hi if b == "hi" else wffn_lo
                            for fc in range(8):
                                mms.append(lambda e, f, l2, a_=asrc, w=wsrc, fc=fc, g=g:
                                           e.matmul(out=PS[4 + g][:, 0:512],
                                                    lhsT=a_[:, 128 * fc:128 * fc + 128],
                                                    rhs=w[:, 1024 * fc + 512 * g:
                                                          1024 * fc + 512 * g + 512],
                                                    start=f, stop=l2))
                        st[f"f_tv{g}"] = mm_group(
                            4 + g, mms, extra_waits=[st["x1f_ready"], ffn_ready])
                        if g == 1:
                            x1fbuf_free.clear()
                            x1fbuf_free.update({"tv": st["f_tv1"]})
                    return t45

                def t6():
                    fv = None
                    for g in range(2):
                        W("vector", {"tv": st[f"f_tv{g}"]})
                        E("vector", lambda e, g=g: e.tensor_tensor(
                            out=x1pre[:, 512 * g:512 * g + 512], in0=PS[4 + g][:, 0:512],
                            in1=biasb[:, 512 * g:512 * g + 512], op=ALU.add), "vv")
                        fv = E("vector", lambda e, g=g: e.tensor_tensor(
                            out=x1pre[:, 512 * g:512 * g + 512],
                            in0=x1pre[:, 512 * g:512 * g + 512],
                            in1=x1_tm[:, 512 * g:512 * g + 512], op=ALU.add), "vv")
                        bankfree[4 + g] = {"vv": fv}
                    W("vector", out_free)
                    ln2_v = emit_ln(x1pre[:], out_sb[:], 1)
                    W("gpsimd", {"vv": ln2_v})
                    stv = E("gpsimd", lambda e, p_=p_, l_=l_: e.dma_start(
                        out=out_h[TP * p_ + 128 * l_:TP * p_ + 128 * l_ + 128, :],
                        in_=out_sb[:]), "st", 16)
                    out_free.clear()
                    out_free.update({"st": stv})
                    stcnt_box[0] = stv

                return [t01, t2, t3, mk_ffn(0), mk_ffn(1), t6]

            def run_tile(p, l, tail, t6_prev):
                nonlocal_state = {}
                tau = NT * p + l
                mpar = tau % 2
                W("sync", mask_free[mpar])
                ldm = E("sync", lambda e, tau=tau, mpar=mpar: e.dma_start(
                    out=mask_sb[mpar][:], in_=masks_h[tau, :, :]), f"lm{mpar}", 16)
                ldmsem = f"lm{mpar}"
                ldx_box = [None]

                s_tv = {}
                tr_tvs = {}
                av_tvs = {}
                ex_ss = {}
                ph_ss = {}
                pt_ss = {}
                ao_ss = {}
                attn_ready = {}

                def emit_S(h):
                    hp, hb = h // 2, h % 2
                    sb = hb
                    qs = 512 * hp + 128 * l
                    ks = 640 * hp + 128 * l
                    qsl = q_f[64 * hb:64 * hb + 64, qs:qs + 128]
                    gsl = gk_f[64 * hb:64 * hb + 64, 64 * hp:64 * hp + 64]
                    ksl = k_f[64 * hb:64 * hb + 64, ks:ks + 256]
                    mms = [lambda e, f, l2, sb=sb, mpar=mpar: e.matmul(
                        out=PS[sb][:, 0:NKEY], lhsT=idb[:], rhs=mask_sb[mpar][:],
                        start=True, stop=False)]
                    mms.append(lambda e, f, l2, q=qsl, g=gsl, sb=sb: e.matmul(
                        out=PS[sb][:, 0:64], lhsT=q, rhs=g, start=False, stop=False))
                    mms.append(lambda e, f, l2, q=qsl, k=ksl, sb=sb: e.matmul(
                        out=PS[sb][:, 64:NKEY], lhsT=q, rhs=k, start=False, stop=l2))
                    extra = [{ldmsem: ldm}]
                    if h < 2 and l == 0:
                        extra += [q_ready, k_ready, gkv_ready]
                    s_tv[h] = mm_group(sb, mms, extra_waits=extra)

                emit_S(0)
                emit_S(1)

                for slot in range(20):
                    for _f in range(3):
                        if fillers:
                            fillers.pop(0)()
                    if 3 <= slot <= H + 2:
                        hh_ = slot - 3
                        pr_ = hh_ % 2
                        W("vector", {"tv": tr_tvs[hh_]})
                        W("vector", ptbuf_free[pr_])
                        pt_ss[hh_] = E("vector", lambda e, pr_=pr_: e.tensor_copy(
                            out=pt_f[pr_][:], in_=PS[2 + pr_][:, 0:384]), "vv")
                        bankfree[2 + pr_] = {"vv": pt_ss[hh_]}
                    if 4 <= slot <= H + 3:
                        ho_ = slot - 4
                        hp_, hb_ = ho_ // 2, ho_ % 2
                        ob_ = 6 + ho_ % 2
                        W("scalar", {"tv": av_tvs[ho_]})
                        W("scalar", attnfree_chunk[hp_])
                        ao_ss[ho_] = E("scalar", lambda e, hb=hb_, hp=hp_, ob=ob_: e.copy(
                            out=attn_hi[64 * hb:64 * hb + 64, 128 * hp:128 * hp + 128],
                            in_=PS[ob][0:64, 0:128]), "sv")
                    if slot == 2:
                        if t6_prev:
                            t6_prev[0]()
                        if tail:
                            tail[0]()
                    if tail and slot == 7:
                        tail[1]()
                    if slot == 9:
                        W("sync", xres_free)
                        ldx_box[0] = E("sync", lambda e, p=p, l=l: e.dma_start(
                            out=xres[:],
                            in_=x_h[TP * p + 128 * l:TP * p + 128 * l + 128, :]),
                            "lr", 16)
                    if tail:
                        if slot == 11:
                            tail[2]()
                        elif slot == 13:
                            tail[3]()
                        elif slot == 14:
                            tail[4]()
                    if slot < H:
                        h = slot
                        sb = h % 2
                        W("vector", {"tv": s_tv[h]})
                        nm_v = E("vector", lambda e, sb=sb, h=h: e.reduce_max(
                            out=negm[:, h:h + 1], in_=PS[sb][:, 0:NKEY], axis=AX.X,
                            negate=True), "vv")
                        W("scalar", {"vv": nm_v})
                        W("scalar", ehb_free[h % 2]) if False else None
                        W("scalar", ehb_free[h % 2])
                        ex_s = E("scalar", lambda e, sb=sb, h=h: e.activation(
                            out=e_sb[h % 2][:], in_=PS[sb][:, 0:NKEY], func=AF.Exp,
                            bias=negm[:, h:h + 1], scale=1.0,
                            accum_out=sumv[:, h:h + 1]), "sv")
                        ex_ss[h] = ex_s
                        bankfree[sb] = {"sv": ex_s, "vv": nm_v}
                    if 1 <= slot <= H:
                        h = slot - 1
                        W("scalar", {"sv": ex_ss[h]})
                        a_ln = E("scalar", lambda e, h=h: e.activation(
                            out=recv[:, h:h + 1], in_=sumv[:, h:h + 1], func=AF.Ln,
                            bias=0.0, scale=1.0), "sv")
                        W("scalar", {"sv": a_ln})
                        a_rs = E("scalar", lambda e, h=h: e.activation(
                            out=sumv[:, h:h + 1], in_=recv[:, h:h + 1], func=AF.Exp,
                            bias=0.0, scale=-1.0), "sv")
                        W("scalar", {"sv": a_rs})
                        W("scalar", phl_free[h % 2])
                        ph_s = E("scalar", lambda e, h=h: e.activation(
                            out=p_f[h % 2][:], in_=e_sb[h % 2][:], func=AF.Identity,
                            bias=0.0, scale=sumv[:, h:h + 1]), "sv")
                        ehb_free[h % 2] = {"sv": ph_s}
                        ph_ss[h] = ph_s
                    if 2 <= slot <= H + 1:
                        h = slot - 2
                        ptb = 2 + h % 2
                        W("tensor", bankfree[ptb])
                        W("tensor", {"sv": ph_ss[h]})
                        tr_tv = None
                        for ci, c0 in enumerate((0, 64, 192)):
                            def mkt(e, c0=c0, ci=ci, ptb=ptb, h=h):
                                return e.transpose(
                                    out=PS[ptb][:, 128 * ci:128 * ci + 128],
                                    in_=p_f[h % 2][:, c0:c0 + 128], identity=idf[:])
                            tr_tv = E("tensor", mkt, "tv" if ci == 2 else None)
                        tr_tvs[h] = tr_tv
                        phl_free[h % 2] = {"tv": tr_tv}
                    if 3 <= slot <= H + 2:
                        h = slot - 3
                        pr = h % 2
                        pt_s = pt_ss[h]
                        ob = 6 + pr
                        mms = []
                        for ci in range(3):
                            if ci == 0:
                                vsl = gv_f[0:64, 64 * h:64 * h + 64]
                                prt = 64
                            else:
                                j = l + ci - 1
                                vsl = v_f[:, 1024 * j + 64 * h:1024 * j + 64 * h + 64]
                                prt = 128
                            mms.append(lambda e, f, l2, v=vsl, pr=pr, ci=ci, prt=prt,
                                       ob=ob: e.matmul(
                                           out=PS[ob][0:64, 0:128], lhsT=v,
                                           rhs=pt_f[pr][0:prt, 128 * ci:128 * ci + 128],
                                           start=f, stop=l2))
                        av_extra = [{"vv": pt_s}]
                        if h == 0 and l == 0:
                            av_extra.append(v_ready)
                        av_tvs[h] = mm_group(ob, mms, extra_waits=av_extra)
                        ptbuf_free[pr] = {"tv": av_tvs[h]}
                    if 4 <= slot <= H + 3:
                        h = slot - 4
                        hp2, hb2 = h // 2, h % 2
                        ob = 6 + h % 2
                        ao_s = ao_ss[h]
                        W("vector", {"sv": ao_s})
                        W("vector", attnfree_chunk[hp2])
                        ao_v = E("vector", lambda e, hb=hb2, hp=hp2, ob=ob: e.tensor_tensor(
                            out=attn_lo[64 * hb:64 * hb + 64, 128 * hp:128 * hp + 128],
                            in0=PS[ob][0:64, 0:128],
                            in1=attn_hi[64 * hb:64 * hb + 64, 128 * hp:128 * hp + 128],
                            op=ALU.subtract), "vv")
                        bankfree[ob] = {"sv": ao_s, "vv": ao_v}
                        _merge(attn_ready, {"sv": ao_s, "vv": ao_v})
                    if slot + 2 < H:
                        emit_S(slot + 2)

                mask_free[mpar] = {"tv": s_tv[H - 1]}
                passend["attn_tv"] = max(s_tv[H - 1], av_tvs[H - 1])
                return make_tail(p, l, ldx_box[0], attn_ready)

            for l in range(NT):
                new_tail = run_tile(p, l, pending_tail[0], pending_t6[0])
                pending_t6[0] = [pending_tail[0][5]] if pending_tail[0] else None
                pending_tail[0] = new_tail
            assert not fillers, "weight-prep fillers must drain within the pass"
            if p == 0:
                _merge(xfm_free, ffn_ready)

            kvbuf_free = {"tv": passend["attn_tv"]}
            xfm_free = {"tv": qkv_last_tv}
            xpf_free = dict(xfm_free)


        if pending_t6[0] is not None:
            for stage in pending_t6[0]:
                stage()
            pending_t6[0] = None
        if pending_tail[0] is not None:
            for stage in pending_tail[0]:
                stage()
            pending_tail[0] = None
        W("gpsimd", {"st": stcnt_box[0]})

        @block.sync
        def _(eng):
            for f in Q["sync"]:
                f(eng)

        @block.gpsimd
        def _(eng):
            for f in Q["gpsimd"]:
                f(eng)

        @block.tensor
        def _(eng):
            for f in Q["tensor"]:
                f(eng)

        @block.vector
        def _(eng):
            for f in Q["vector"]:
                f(eng)

        @block.scalar
        def _(eng):
            for f in Q["scalar"]:
                f(eng)

    return nc


def _build_masks(half):
    r = np.arange(128)[:, None]
    c = np.arange(256)[None, :]
    win_generic = (r <= c) & (c <= r + 128)
    m = np.full((16, 128, NKEY), MASKV, np.float32)
    for t in range(16):
        m[t, :, 0:64] = 0.0
        m[t, :, 64:NKEY][win_generic] = 0.0
    if half == 0:
        m[0, :, 0:64] = MASKV
        m[0, :, 64:NKEY] = MASKV
        m[0, :, 64:NKEY][(c >= 128) & ((c - 128) <= r)] = 0.0
        m[1, :, 0:64] = MASKV
        m[1, :, 64:NKEY] = MASKV
        m[1, :, 64:NKEY][(c <= 128 + r) & ((c < 64) | (c >= r))] = 0.0
    return np.ascontiguousarray(m.astype(BF))


def _host_inputs(core, L0_attn_x, L0_attn_w_qkv, L0_attn_w_out,
                 L0_ffn_G1_R, L0_ffn_G2_R, L0_ffn_G1_I, L0_ffn_G2_I,
                 L0_ffn_bias_real, L0_ffn_bias_imag):
    b, half = core // 2, core % 2
    X = np.asarray(L0_attn_x, np.float32)
    x = np.ascontiguousarray(X[b, RB * half:RB * (half + 1)])
    xpre = np.zeros((192, D), np.float32)
    xpre[0:64] = X[b, 0:64]
    if half == 1:
        xpre[64:192] = X[b, RB - 128:RB]

    def tt_bcast(g1, g2):
        g1 = np.asarray(g1, np.float32)
        g2 = np.asarray(g2, np.float32)
        t = np.transpose(g1[0], (1, 2, 0))
        U = np.broadcast_to(t[:, :, None, :, None],
                            (4, 32, 16, 32, 16)).reshape(4, 512, 512)
        u = np.transpose(g2[:, :, 0, :], (0, 2, 1))
        V = np.broadcast_to(u[:, None, :, None, :],
                            (4, 32, 16, 32, 16)).reshape(4, 512, 512)
        return U, V

    UR, VR = tt_bcast(L0_ffn_G1_R, L0_ffn_G2_R)
    UI, VI = tt_bcast(L0_ffn_G1_I, L0_ffn_G2_I)
    ttu = np.ascontiguousarray(np.concatenate([UR, UI], 0))
    ttv = np.ascontiguousarray(np.concatenate([VR, VI], 0))
    bias_cat = np.concatenate([np.asarray(L0_ffn_bias_real, np.float32),
                               np.asarray(L0_ffn_bias_imag, np.float32)])
    biasb = np.ascontiguousarray(np.broadcast_to(bias_cat[None, :], (128, D)))
    return dict(
        x=x, xpre=xpre,
        wqkv=np.ascontiguousarray(np.asarray(L0_attn_w_qkv, np.float32)),
        wout=np.ascontiguousarray(np.asarray(L0_attn_w_out, np.float32)),
        ttu=ttu, ttv=ttv, biasb=biasb,
        masks=_build_masks(half),
        idf=np.eye(128, dtype=np.float32),
        idb=np.eye(128, dtype=BF),
    )


def kernel(**inputs):
    nc = build_program()
    in_maps = [_host_inputs(c, **inputs) for c in range(NCORE)]
    res = run_bass_kernel_spmd(nc, in_maps, core_ids=list(range(NCORE)))
    out = np.empty((B, S, D), np.float32)
    for c in range(NCORE):
        b, half = c // 2, c % 2
        out[b, RB * half:RB * (half + 1)] = res.results[c]["out"]
    return out


# version 25
# speedup vs baseline: 1.0141x; 1.0141x over previous
import contextlib
import numpy as np
import ml_dtypes
import concourse.bass as bass
import concourse.mybir as mybir
from concourse.bass_utils import run_bass_kernel_spmd

F32 = mybir.dt.float32
BF16 = mybir.dt.bfloat16
AF = mybir.ActivationFunctionType
ALU = mybir.AluOpType
AX = mybir.AxisListType
BF = ml_dtypes.bfloat16

B, S, D, H = 4, 4096, 1024, 16
NCORE = 8
RB = 2048
NP, TP = 4, 512
NT = TP // 128
NKEY = 320
MASKV = -50.0
EPS = 1e-5
ES = (("hi", "hi"), ("hi", "lo"), ("lo", "hi"))
ENGS = ("sync", "gpsimd", "tensor", "vector", "scalar")
LIMIT = None
SKIP_TT = False


class _Stop(Exception):
    pass


def _merge(d, upd):
    for k, v in upd.items():
        if v and v > d.get(k, 0):
            d[k] = v


def build_program():
    nc = bass.Bass("TRN2", target_bir_lowering=False, detect_race_conditions=False)

    x_h = nc.dram_tensor("x", [RB, D], F32, kind="ExternalInput").ap()
    xpre_h = nc.dram_tensor("xpre", [192, D], F32, kind="ExternalInput").ap()
    wqkv_h = nc.dram_tensor("wqkv", [D, 3 * D], F32, kind="ExternalInput").ap()
    wout_h = nc.dram_tensor("wout", [D, D], F32, kind="ExternalInput").ap()
    ttu_h = nc.dram_tensor("ttu", [8, 512, 512], F32, kind="ExternalInput").ap()
    ttv_h = nc.dram_tensor("ttv", [8, 512, 512], F32, kind="ExternalInput").ap()
    biasb_h = nc.dram_tensor("biasb", [128, D], F32, kind="ExternalInput").ap()
    masks_h = nc.dram_tensor("masks", [16, 128, NKEY], BF16, kind="ExternalInput").ap()
    idf_h = nc.dram_tensor("idf", [128, 128], F32, kind="ExternalInput").ap()
    idb_h = nc.dram_tensor("idb", [128, 128], BF16, kind="ExternalInput").ap()
    out_h = nc.dram_tensor("out", [RB, D], F32, kind="ExternalOutput").ap()

    with contextlib.ExitStack() as ctx:
        A = ctx.enter_context

        idf = A(nc.sbuf_tensor("s_idf", [128, 128], F32))
        idb = A(nc.sbuf_tensor("s_idb", [128, 128], BF16))
        biasb = A(nc.sbuf_tensor("s_biasb", [128, D], F32))
        wout_hi = A(nc.sbuf_tensor("s_wout_hi", [128, 8 * 1024], BF16))
        wout_lo = A(nc.sbuf_tensor("s_wout_lo", [128, 8 * 1024], BF16))
        wffn_hi = A(nc.sbuf_tensor("s_wffn_hi", [128, 8 * 1024], BF16))
        wffn_lo = A(nc.sbuf_tensor("s_wffn_lo", [128, 8 * 1024], BF16))
        wbuf = [A(nc.sbuf_tensor(f"s_wbuf{i}", [128, 512], F32)) for i in range(3)]
        whi = [A(nc.sbuf_tensor(f"s_whi{i}", [128, 512], BF16)) for i in range(3)]
        wlo = [A(nc.sbuf_tensor(f"s_wlo{i}", [128, 512], BF16)) for i in range(3)]
        gk_f = A(nc.sbuf_tensor("s_gk_f", [128, 8 * 64], F32))
        gv_f = A(nc.sbuf_tensor("s_gv_f", [64, 1024], F32))
        x_tm = [A(nc.sbuf_tensor(f"s_xtm{i}", [128, D], F32)) for i in range(2)]
        xf_hi = A(nc.sbuf_tensor("s_xf_hi", [128, 8 * 640], BF16))
        xf_lo = A(nc.sbuf_tensor("s_xf_lo", [128, 8 * 640], BF16))
        xp_hi = A(nc.sbuf_tensor("s_xp_hi", [128, 8 * 64], BF16))
        xp_lo = A(nc.sbuf_tensor("s_xp_lo", [128, 8 * 64], BF16))
        q_f = A(nc.sbuf_tensor("s_q_f", [128, 8 * 512], F32))
        k_f = A(nc.sbuf_tensor("s_k_f", [128, 8 * 640], F32))
        v_f = A(nc.sbuf_tensor("s_v_f", [128, 5 * 1024], F32))
        mask_sb = [A(nc.sbuf_tensor(f"s_mask{i}", [128, NKEY], BF16)) for i in range(2)]
        e_sb = [A(nc.sbuf_tensor(f"s_e{i}", [128, NKEY], F32)) for i in range(2)]
        p_f = [A(nc.sbuf_tensor(f"s_p_f{i}", [128, NKEY], F32)) for i in range(2)]
        pt_f = [A(nc.sbuf_tensor(f"s_pt_f{i}", [128, 384], F32)) for i in range(2)]
        attn_hi = A(nc.sbuf_tensor("s_attn_hi", [128, 8 * 128], BF16))
        attn_lo = A(nc.sbuf_tensor("s_attn_lo", [128, 8 * 128], BF16))
        negm = A(nc.sbuf_tensor("s_negm", [128, 16], F32))
        sumv = A(nc.sbuf_tensor("s_sumv", [128, 16], F32))
        recv = A(nc.sbuf_tensor("s_recv", [128, 16], F32))
        xres = A(nc.sbuf_tensor("s_xres", [128, D], F32))
        x1pre = A(nc.sbuf_tensor("s_x1pre", [128, D], F32))
        x1_tm = A(nc.sbuf_tensor("s_x1tm", [128, D], F32))
        x1f_hi = A(nc.sbuf_tensor("s_x1f_hi", [128, 8 * 128], BF16))
        x1f_lo = A(nc.sbuf_tensor("s_x1f_lo", [128, 8 * 128], BF16))
        out_sb = A(nc.sbuf_tensor("s_out0", [128, D], F32))
        ln_st = [A(nc.sbuf_tensor("s_lnst0", [128, 12], F32))] * 2
        ln_sqm = [A(nc.sbuf_tensor("s_lnsqm0", [128, 4], F32))] * 2
        ln_r = [A(nc.sbuf_tensor("s_lnr0", [128, 8], F32))] * 2
        ln_sq = [A(nc.sbuf_tensor("s_lnsq0", [128, 1], F32))] * 2

        PS = [A(nc.psum_tensor(f"ps{i}", [128, 512], F32)) for i in range(8)]

        semh = {n: A(nc.semaphore(f"sem_{n}"))
                for n in ("ld", "st", "tv", "vv", "sv", "lc", "lt",
                          "lw0", "lw1", "lw2", "lx0", "lx1", "lr", "lm0", "lm1")}
        block = A(nc.Block())

        tta = out_sb[:, 0:512]
        ttt = out_sb[:, 512:1024]
        xf_f32 = xf_hi.bitcast(F32)

        Q = {e: [] for e in ENGS}
        cnt = {n: 0 for n in semh}
        nemit = [0]

        def E(eng, make, inc=None, n=1):
            nemit[0] += 1
            off = LIMIT is not None and nemit[0] > LIMIT
            if inc is None:
                if not off:
                    Q[eng].append(make)
                return None
            cnt[inc] += n
            h = semh[inc]
            if not off:
                Q[eng].append(lambda e, mk=make, h=h, n=n: mk(e).then_inc(h, n))
            return cnt[inc]

        def W(eng, waits):
            if LIMIT is not None and nemit[0] >= LIMIT:
                return
            for s, v in waits.items():
                if v:
                    Q[eng].append(lambda e, h=semh[s], v=v: e.wait_ge(h, v))

        bankfree = {i: {} for i in range(8)}

        def mm_group(bank, mms, extra_waits=()):
            W("tensor", bankfree[bank])
            for w in extra_waits:
                W("tensor", w)
            tvv = None
            for i, mk in enumerate(mms):
                first, last = i == 0, i == len(mms) - 1
                tvv = E("tensor", (lambda e, mk=mk, f=first, l=last: mk(e, f, l)),
                        "tv" if last else None)
            return tvv

        E("sync", lambda e: e.dma_start(out=idf[:], in_=idf_h[:]), "lc", 16)
        E("sync", lambda e: e.dma_start(out=idb[:], in_=idb_h[:]), "lc", 16)
        E("sync", lambda e: e.dma_start(out=biasb[:], in_=biasb_h[:]), "lc", 16)
        consts_ready = {"lc": 48}

        ffn_ready = {}
        wout_ready = {}
        _prep = {}

        def build_weight_prep_fillers(xfree):
            fillers = []
            st = {"tt_mult_done": {0: dict(xfree), 1: dict(xfree)}, "it": 0,
                  "accv": None}

            def mk_ttc(qq, ri, c):
                def go():
                    idx = ri * 4 + c
                    par = st["it"] % 2
                    st["it"] += 1
                    ttu = xf_f32[:, 1024 * par:1024 * par + 512]
                    ttv = xf_f32[:, 1024 * par + 512:1024 * par + 1024]
                    W("sync", st["tt_mult_done"][par])
                    E("sync", lambda e, idx=idx, qq=qq, ttu=ttu: e.dma_start(
                        out=ttu, in_=ttu_h[idx, 128 * qq:128 * (qq + 1), :]), "lt", 16)
                    l2 = E("sync", lambda e, idx=idx, qq=qq, ttv=ttv: e.dma_start(
                        out=ttv, in_=ttv_h[idx, 128 * qq:128 * (qq + 1), :]), "lt", 16)
                    W("vector", {"lt": l2})
                    if c == 0:
                        W("vector", st.get("plc_sv", {}))
                        st["accv"] = E("vector", lambda e, ttu=ttu, ttv=ttv: e.tensor_tensor(
                            out=tta, in0=ttu, in1=ttv, op=ALU.mult), "vv")
                    else:
                        E("vector", lambda e, ttu=ttu, ttv=ttv: e.tensor_tensor(
                            out=ttt, in0=ttu, in1=ttv, op=ALU.mult), "vv")
                        st["accv"] = E("vector", lambda e: e.tensor_tensor(
                            out=tta, in0=tta, in1=ttt, op=ALU.add), "vv")
                    st["tt_mult_done"][par] = {"vv": st["accv"]}
                return go

            def mk_plc(sl, coff, sgn):
                def go():
                    dh = wffn_hi[:, 1024 * sl + coff:1024 * sl + coff + 512]
                    dl = wffn_lo[:, 1024 * sl + coff:1024 * sl + coff + 512]
                    W("scalar", {"vv": st["accv"]})
                    lsv = E("scalar", lambda e, dh=dh, sgn=sgn: e.activation(
                        out=dh, in_=tta, func=AF.Identity, bias=0.0, scale=sgn), "sv")
                    W("vector", {"sv": lsv})
                    lvv = E("vector", lambda e, dh=dh, dl=dl, sgn=sgn:
                            e.scalar_tensor_tensor(out=dl, in0=tta, scalar=sgn, in1=dh,
                                                   op0=ALU.mult, op1=ALU.subtract), "vv")
                    st["plc_sv"] = {"sv": lsv}
                    _merge(ffn_ready, {"sv": lsv, "vv": lvv})
                return go

            for qq in range(4 if not SKIP_TT else 0):
                for ri in range(2):
                    for c in range(4):
                        fillers.append(mk_ttc(qq, ri, c))
                    plc = ([(qq, 0, 1.0), (4 + qq, 512, 1.0)] if ri == 0
                           else [(qq, 512, 1.0), (4 + qq, 0, -1.0)])
                    for sl, coff, sgn in plc:
                        fillers.append(mk_plc(sl, coff, sgn))
            return fillers

        def build_wout_fillers():
            def mk(pc):
                def go():
                    fc, g = pc // 2, pc % 2
                    db = wpiece[0] % 3
                    wpiece[0] += 1
                    W("sync", wdb_free[db])
                    ldv = E("sync", lambda e, fc=fc, g=g, db=db: e.dma_start(
                        out=wbuf[db][:],
                        in_=wout_h[128 * fc:128 * (fc + 1),
                                   512 * g:512 * g + 512]), f"lw{db}", 16)
                    W("sync", wuse_free[db])
                    c0 = 1024 * fc + 512 * g
                    W("scalar", {f"lw{db}": ldv})
                    W("scalar", wuse_free[db])
                    sv = E("scalar", lambda e, c0=c0, db=db: e.copy(
                        out=wout_hi[:, c0:c0 + 512], in_=wbuf[db][:]), "sv")
                    W("vector", {f"lw{db}": ldv, "sv": sv})
                    vv = E("vector", lambda e, c0=c0, db=db: e.tensor_tensor(
                        out=wout_lo[:, c0:c0 + 512], in0=wbuf[db][:],
                        in1=wout_hi[:, c0:c0 + 512], op=ALU.subtract), "vv")
                    wdb_free[db] = {"sv": sv, "vv": vv}
                    _merge(wout_ready, {"sv": sv, "vv": vv})
                return go
            return [mk(pc) for pc in range(16)]

        wdb_free = {0: {}, 1: {}, 2: {}}
        wuse_free = {0: {}, 1: {}, 2: {}}
        xpf_ready = {}
        xpf_free = {}
        gkv_ready = {}
        kvbuf_free = {}
        xfm_free = {}
        mask_free = [{}, {}]
        xres_free = {}
        xtm_free = [{}, {}]
        ptbuf_free = [{}, {}]
        out_free = {}
        ehb_free = [{}, {}]
        phl_free = [{}, {}]
        attnfree_chunk = [{} for _ in range(8)]
        x1fbuf_free = {}
        _lnb = {}
        ln_busy = [_lnb, _lnb]
        wpiece = [0]
        fillers = []
        stcnt_box = [0]
        pending_tail = [None]
        pending_t6 = [None]
        passend = {"attn_tv": 0}

        def emit_ln(src_, dst_, si):
            W("vector", {"vv": cnt["vv"]})
            W("vector", ln_busy[si])
            E("vector", lambda e, si=si, src_=src_: e.bn_stats(
                out=ln_st[si][:, 0:6], in_=src_[:, 0:512]), "vv")
            v2 = E("vector", lambda e, si=si, src_=src_: e.bn_stats(
                out=ln_st[si][:, 6:12], in_=src_[:, 512:1024]), "vv")
            W("scalar", {"vv": v2})
            E("scalar", lambda e, si=si: e.activation(
                out=ln_sqm[si][:], in_=ln_st[si][:, 1:12:3], func=AF.Identity,
                bias=0.0, scale=1.0, accum_out=ln_r[si][:, 0:1]), "sv")
            E("scalar", lambda e, si=si: e.activation(
                out=ln_sqm[si][:], in_=ln_st[si][:, 2:12:3], func=AF.Identity,
                bias=0.0, scale=1.0, accum_out=ln_r[si][:, 1:2]), "sv")
            a3 = E("scalar", lambda e, si=si: e.activation(
                out=ln_sqm[si][:], in_=ln_st[si][:, 1:12:3], func=AF.Square,
                bias=0.0, scale=1.0, accum_out=ln_r[si][:, 2:3]), "sv")
            W("vector", {"sv": a3})
            E("vector", lambda e, si=si: e.tensor_scalar(
                out=ln_r[si][:, 3:4], in0=ln_r[si][:, 0:1], scalar1=0.25,
                scalar2=None, op0=ALU.mult), "vv")
            d2 = E("vector", lambda e, si=si: e.tensor_scalar(
                out=ln_r[si][:, 4:5], in0=ln_r[si][:, 1:2], scalar1=1.0 / 1024.0,
                scalar2=EPS, op0=ALU.mult, op1=ALU.add), "vv")
            W("vector", {"vv": d2})
            E("vector", lambda e, si=si: e.scalar_tensor_tensor(
                out=ln_r[si][:, 5:6], in0=ln_r[si][:, 2:3], scalar=0.25,
                in1=ln_r[si][:, 4:5], op0=ALU.mult, op1=ALU.add), "vv")
            d4 = E("vector", lambda e, si=si: e.tensor_tensor(
                out=ln_r[si][:, 6:7], in0=ln_r[si][:, 3:4], in1=ln_r[si][:, 3:4],
                op=ALU.mult), "vv")
            W("vector", {"vv": d4})
            d5 = E("vector", lambda e, si=si: e.tensor_tensor(
                out=ln_r[si][:, 7:8], in0=ln_r[si][:, 5:6], in1=ln_r[si][:, 6:7],
                op=ALU.subtract), "vv")
            W("scalar", {"vv": d5})
            a4 = E("scalar", lambda e, si=si: e.activation(
                out=ln_sq[si][:], in_=ln_r[si][:, 7:8], func=AF.Ln,
                bias=0.0, scale=1.0), "sv")
            W("scalar", {"sv": a4})
            a5 = E("scalar", lambda e, si=si: e.activation(
                out=ln_r[si][:, 4:5], in_=ln_sq[si][:], func=AF.Exp,
                bias=0.0, scale=-0.5), "sv")
            W("vector", {"sv": a5})
            d7 = E("vector", lambda e, si=si, src_=src_, dst_=dst_: e.tensor_scalar(
                out=dst_, in0=src_, scalar1=ln_r[si][:, 3:4],
                scalar2=ln_r[si][:, 4:5], op0=ALU.subtract, op1=ALU.mult), "vv")
            ln_busy[si].clear()
            ln_busy[si].update({"sv": a5})
            return d7

        for p in range(NP):
            if pending_t6[0] is not None:
                for stage in pending_t6[0]:
                    stage()
                pending_t6[0] = None
            if pending_tail[0] is not None:
                for stage in pending_tail[0]:
                    stage()
                pending_tail[0] = None
            pfx_ready = {}
            if p > 0:
                W("vector", kvbuf_free)
                E("vector", lambda e: e.tensor_copy(
                    out=k_f[:].rearrange("p (c n) -> p c n", n=640)[:, :, 0:128],
                    in_=k_f[:].rearrange("p (c n) -> p c n", n=640)[:, :, 512:640]), "vv")
                a4 = E("vector", lambda e: e.tensor_copy(
                    out=v_f[:, 0:1024], in_=v_f[:, 4096:5120]), "vv")
                pfx_ready = {"vv": a4}

            xfm_ready = {}
            jobs = [("own", t) for t in range(4)]
            if p == 0:
                jobs += [("halo", 0), ("glob", 0)]
            for ji, (kind, t) in enumerate(jobs):
                db = ji % 2
                W("sync", xtm_free[db])
                if kind == "own":
                    ldv = E("sync", lambda e, p=p, t=t, db=db: e.dma_start(
                        out=x_tm[db][:],
                        in_=x_h[TP * p + 128 * t:TP * p + 128 * t + 128, :]), f"lx{db}", 16)
                    nrow = 128
                elif kind == "halo":
                    ldv = E("sync", lambda e, db=db: e.dma_start(
                        out=x_tm[db][:], in_=xpre_h[64:192, :]), f"lx{db}", 16)
                    nrow = 128
                else:
                    ldv = E("sync", lambda e, db=db: e.dma_start(
                        out=x_tm[db][0:64, :], in_=xpre_h[0:64, :]), f"lx{db}", 16)
                    nrow = 64
                ldsem = f"lx{db}"
                tvl = None
                for half4 in range(2):
                    bank = half4
                    W("tensor", bankfree[bank])
                    if half4 == 0:
                        W("tensor", {ldsem: ldv})
                        if p == 0 and ji == 0:
                            W("tensor", consts_ready)
                    for sl in range(4):
                        dk = 4 * half4 + sl
                        def mkt(e, db=db, dk=dk, bank=bank, sl=sl, nrow=nrow):
                            return e.transpose(
                                out=PS[bank][:, 128 * sl:128 * sl + nrow],
                                in_=x_tm[db][0:nrow, 128 * dk:128 * dk + 128],
                                identity=idf[0:nrow, 0:nrow])
                        tvl = E("tensor", mkt, "tv" if sl == 3 else None)
                    W("scalar", {"tv": tvl})
                    W("vector", {"tv": tvl})
                    if kind == "own" and p > 0:
                        W("scalar", xfm_free)
                        W("vector", xfm_free)
                    slots = []
                    for sl in range(4):
                        dk = 4 * half4 + sl
                        if kind == "own":
                            c0 = 640 * dk + 128 + 128 * t
                            dh = xf_hi[:, c0:c0 + 128]
                            dl = xf_lo[:, c0:c0 + 128]
                            ncol = 128
                        elif kind == "halo":
                            dh = xf_hi[:, 640 * dk:640 * dk + 128]
                            dl = xf_lo[:, 640 * dk:640 * dk + 128]
                            ncol = 128
                        else:
                            dh = xp_hi[:, 64 * dk:64 * dk + 64]
                            dl = xp_lo[:, 64 * dk:64 * dk + 64]
                            ncol = 64
                        slots.append((sl, dh, dl, ncol))
                    evs = evv = None
                    for sl, dh, dl, ncol in slots:
                        evs = E("scalar", lambda e, bank=bank, sl=sl, dh=dh, ncol=ncol: e.copy(
                            out=dh, in_=PS[bank][:, 128 * sl:128 * sl + ncol]), "sv")
                    W("vector", {"sv": evs})
                    for sl, dh, dl, ncol in slots:
                        evv = E("vector", lambda e, bank=bank, sl=sl, dh=dh, dl=dl, ncol=ncol:
                                e.tensor_tensor(out=dl,
                                                in0=PS[bank][:, 128 * sl:128 * sl + ncol],
                                                in1=dh, op=ALU.subtract), "vv")
                    bankfree[bank] = {"sv": evs, "vv": evv}
                xtm_free[db] = {"tv": tvl}
                if kind == "own":
                    _merge(xfm_ready, {"sv": evs, "vv": evv})
                else:
                    _merge(xpf_ready, {"sv": evs, "vv": evv})

            def load_wpiece(dma_make):
                db = wpiece[0] % 3
                wpiece[0] += 1
                W("sync", wdb_free[db])
                ldv = E("sync", lambda e, db=db, mk=dma_make: mk(e, db), f"lw{db}", 16)
                W("scalar", {f"lw{db}": ldv})
                W("scalar", wuse_free[db])
                sv = E("scalar", lambda e, db=db: e.copy(out=whi[db][:], in_=wbuf[db][:]), "sv")
                W("vector", {"sv": sv})
                vv = E("vector", lambda e, db=db: e.tensor_tensor(
                    out=wlo[db][:], in0=wbuf[db][:], in1=whi[db][:], op=ALU.subtract), "vv")
                wdb_free[db] = {"sv": sv, "vv": vv}
                return db, {"sv": sv, "vv": vv}

            q_ready, k_ready = {}, {}
            qkv_last_tv = 0
            def load_qk_pair(sec, fc):
                gcol = (0 if sec == "q" else 1024) + 128 * fc
                dbs, wreadys = [], []
                for hh in range(2):
                    db_, wr_ = load_wpiece(
                        lambda e, db, gcol=gcol, hh=hh: e.dma_start(
                            out=wbuf[db][:].rearrange("p (dk c) -> p dk c", c=128),
                            in_=wqkv_h[512 * hh:512 * hh + 512,
                                       gcol:gcol + 128].rearrange(
                                "(dk p) c -> p dk c", p=128)))
                    dbs.append(db_)
                    wreadys.append(wr_)
                return dbs, wreadys

            qk_jobs = [(sec, fc) for sec in ("q", "k") for fc in range(8)]
            loaded = {qk_jobs[0]: load_qk_pair(*qk_jobs[0])}
            for ji, (sec, fc) in enumerate(qk_jobs):
                    dbs, wreadys = loaded.pop((sec, fc))
                    bank = fc % 2
                    extra = wreadys + [xfm_ready]
                    if p > 0:
                        extra.append(kvbuf_free)
                    mms = []
                    for dk in range(8):
                        for a, b in ES:
                            dbx = dbs[dk // 4]
                            wsrc = (whi if a == "hi" else wlo)[dbx][
                                :, 128 * (dk % 4):128 * (dk % 4) + 128]
                            xsrc = (xf_hi if b == "hi" else xf_lo)[
                                :, 640 * dk + 128:640 * dk + 640]
                            mms.append(lambda e, f, l, w=wsrc, x=xsrc, bank=bank:
                                       e.matmul(out=PS[bank][:, 0:512], lhsT=w, rhs=x,
                                                start=f, stop=l))
                    W("tensor", bankfree[bank])
                    for w_ in extra:
                        W("tensor", w_)
                    tvq = tv_mid = None
                    for i, mk in enumerate(mms):
                        first, last = i == 0, i == len(mms) - 1
                        r = E("tensor", (lambda e, mk=mk, f=first, l=last: mk(e, f, l)),
                              "tv" if (last or i == 11) else None)
                        if i == 11:
                            tv_mid = r
                        if last:
                            tvq = r
                    wuse_free[dbs[0]] = {"tv": tv_mid}
                    qkv_last_tv = tvq
                    wuse = {"tv": tvq}
                    tvp = tvg = None
                    if p == 0 and sec == "k":
                        mmsp, mmsg = [], []
                        for a, b in ES:
                            for dk in range(8):
                                dbx = dbs[dk // 4]
                                wsrc = (whi if a == "hi" else wlo)[dbx][
                                    :, 128 * (dk % 4):128 * (dk % 4) + 128]
                                xfh = xf_hi if b == "hi" else xf_lo
                                xph = xp_hi if b == "hi" else xp_lo
                                mmsp.append(lambda e, f, l, w=wsrc,
                                            x=xfh[:, 640 * dk:640 * dk + 128]:
                                            e.matmul(out=PS[4][:, 0:128], lhsT=w, rhs=x,
                                                     start=f, stop=l))
                                mmsg.append(lambda e, f, l, w=wsrc,
                                            x=xph[:, 64 * dk:64 * dk + 64]:
                                            e.matmul(out=PS[5][:, 0:64], lhsT=w, rhs=x,
                                                     start=f, stop=l))
                        tvp = mm_group(4, mmsp, extra_waits=wreadys + [xpf_ready])
                        tvg = mm_group(5, mmsg, extra_waits=wreadys)
                        qkv_last_tv = tvg
                        wuse = {"tv": tvg}
                        wuse_free[dbs[0]] = wuse
                    wuse_free[dbs[1]] = wuse
                    if ji + 1 < len(qk_jobs):
                        loaded[qk_jobs[ji + 1]] = load_qk_pair(*qk_jobs[ji + 1])
                    if sec == "q":
                        W("scalar", {"tv": tvq})
                        if p > 0:
                            W("scalar", kvbuf_free)
                            W("scalar", pfx_ready)
                        sv1 = E("scalar", lambda e, fc=fc, bank=bank: e.activation(
                            out=q_f[:, 512 * fc:512 * (fc + 1)], in_=PS[bank][:, 0:512],
                            func=AF.Identity, bias=0.0, scale=0.125), "sv")
                        bankfree[bank] = {"sv": sv1}
                        _merge(q_ready, {"sv": sv1})
                    else:
                        W("vector", {"tv": tvq})
                        if p > 0:
                            W("vector", kvbuf_free)
                            W("vector", pfx_ready)
                        sv1 = E("vector", lambda e, fc=fc, bank=bank: e.tensor_copy(
                            out=k_f[:, 640 * fc + 128:640 * fc + 640],
                            in_=PS[bank][:, 0:512]), "vv")
                        bankfree[bank] = {"vv": sv1}
                        _merge(k_ready, {"vv": sv1})
                        if p == 0:
                            W("vector", {"tv": tvg})
                            sv2 = E("vector", lambda e, fc=fc: e.tensor_copy(
                                out=k_f[:, 640 * fc:640 * fc + 128],
                                in_=PS[4][:, 0:128]), "vv")
                            bankfree[4] = {"vv": sv2}
                            sv3 = E("vector", lambda e, fc=fc: e.tensor_copy(
                                out=gk_f[:, 64 * fc:64 * (fc + 1)], in_=PS[5][:, 0:64]), "vv")
                            bankfree[5] = {"vv": sv3}
                            _merge(k_ready, {"vv": max(sv2, sv3)})
                            _merge(gkv_ready, {"vv": sv3})

            v_ready = {}
            vjobs0 = [("own", j, j) for j in range(4)]
            if p == 0:
                vjobs0 += [("pre", None, 4), ("glob", None, 5)]
            for vcol in range(2):
                vjobs = (vjobs0 if vcol == 0 else
                         [(k, t, jb + 2) for (k, t, jb) in vjobs0])
                for _, _, jb in vjobs:
                    W("tensor", bankfree[jb])
                group_last = {jb: None for _, _, jb in vjobs}

                def load_v(dk, vcol=vcol):
                    return load_wpiece(
                        lambda e, db, dk=dk, vcol=vcol: e.dma_start(
                            out=wbuf[db][:],
                            in_=wqkv_h[128 * dk:128 * (dk + 1),
                                       2048 + 512 * vcol:2048 + 512 * vcol + 512]))
                nxt = load_v(0)
                for dk in range(8):
                    db, wready = nxt
                    if dk + 1 < 8:
                        nxt = load_v(dk + 1)
                    W("tensor", wready)
                    if dk == 0:
                        W("tensor", xfm_ready)
                        if p == 0:
                            W("tensor", xpf_ready)
                        if p > 0:
                            W("tensor", kvbuf_free)
                    dk_last_tv = None
                    for ai, (a, b) in enumerate(ES):
                        wsrc = (whi if b == "hi" else wlo)[db][:]
                        for kind, t, jb in vjobs:
                            if kind == "own":
                                c0 = 640 * dk + 128 + 128 * t
                                lh = (xf_hi if a == "hi" else xf_lo)[:, c0:c0 + 128]
                                dst = PS[jb][:, 0:512]
                            elif kind == "pre":
                                lh = (xf_hi if a == "hi" else xf_lo)[
                                    :, 640 * dk:640 * dk + 128]
                                dst = PS[jb][:, 0:512]
                            else:
                                lh = (xp_hi if a == "hi" else xp_lo)[
                                    :, 64 * dk:64 * dk + 64]
                                dst = PS[jb][0:64, 0:512]
                            first = (dk == 0 and ai == 0)
                            last = (dk == 7 and ai == len(ES) - 1)
                            is_dk_last = (ai == len(ES) - 1 and (kind, t, jb) == vjobs[-1])
                            r = E("tensor",
                                  lambda e, lh=lh, w=wsrc, dst=dst, f=first, l=last:
                                  e.matmul(out=dst, lhsT=lh, rhs=w, start=f, stop=l),
                                  "tv" if (is_dk_last or last) else None)
                            if r is not None:
                                dk_last_tv = r
                            if last:
                                group_last[jb] = dk_last_tv
                    wuse_free[db] = {"tv": dk_last_tv}
                last_tv = max(v for v in group_last.values())
                qkv_last_tv = max(qkv_last_tv, last_tv)
                for kind, t, jb in vjobs:
                    W("scalar", {"tv": group_last[jb]})
                    if p > 0:
                        W("scalar", kvbuf_free)
                        W("scalar", pfx_ready)
                    if kind == "own":
                        c0 = 1024 * (t + 1) + 512 * vcol
                        dh = v_f[:, c0:c0 + 512]
                        src = PS[jb][:, 0:512]
                    elif kind == "pre":
                        c0 = 512 * vcol
                        dh = v_f[:, c0:c0 + 512]
                        src = PS[jb][:, 0:512]
                    else:
                        c0 = 512 * vcol
                        dh = gv_f[0:64, c0:c0 + 512]
                        src = PS[jb][0:64, 0:512]
                    sv1 = E("scalar", lambda e, dh=dh, src=src: e.copy(out=dh, in_=src), "sv")
                    bankfree[jb] = {"sv": sv1}
                    _merge(v_ready, {"sv": sv1})
                    if kind == "glob":
                        _merge(gkv_ready, {"sv": sv1})
            _merge(v_ready, pfx_ready)
            _merge(k_ready, pfx_ready)
            if p == 0:
                fillers[:] = build_wout_fillers() + \
                    build_weight_prep_fillers({"tv": qkv_last_tv})


            def make_tail(p_, l_, ldx_, attn_rdy_):
                st = {}

                def t01():
                    W("tensor", bankfree[4])
                    W("tensor", bankfree[5])
                    W("tensor", attn_rdy_)
                    W("tensor", wout_ready)
                    lastfc = None
                    for fc in range(8):
                        for ai, (a, b) in enumerate(ES):
                            asrc = attn_hi if a == "hi" else attn_lo
                            wsrc = wout_hi if b == "hi" else wout_lo
                            first = (fc == 0 and ai == 0)
                            lastm = (fc == 7 and ai == 2)
                            E("tensor", lambda e, a_=asrc, w=wsrc, fc=fc, f=first, l=lastm:
                              e.matmul(out=PS[4][:, 0:512],
                                       lhsT=a_[:, 128 * fc:128 * fc + 128],
                                       rhs=w[:, 1024 * fc:1024 * fc + 512],
                                       start=f, stop=l))
                            lastfc = E("tensor",
                                       lambda e, a_=asrc, w=wsrc, fc=fc, f=first, l=lastm:
                                       e.matmul(out=PS[5][:, 0:512],
                                                lhsT=a_[:, 128 * fc:128 * fc + 128],
                                                rhs=w[:, 1024 * fc + 512:1024 * fc + 1024],
                                                start=f, stop=l),
                                       "tv" if ai == 2 else None)
                        attnfree_chunk[fc] = {"tv": lastfc}
                    st["y_tv"] = lastfc

                def t2():
                    W("vector", {"tv": st["y_tv"], "lr": ldx_})
                    yv = None
                    for g in range(2):
                        yv = E("vector", lambda e, g=g: e.tensor_tensor(
                            out=x1pre[:, 512 * g:512 * g + 512], in0=PS[4 + g][:, 0:512],
                            in1=xres[:, 512 * g:512 * g + 512], op=ALU.add), "vv")
                        bankfree[4 + g] = {"vv": yv}
                    xres_free.clear()
                    xres_free.update({"vv": yv})
                    st["ln1_v"] = emit_ln(x1pre[:], x1_tm[:], 0)

                def t3():
                    x1f_ready = {}
                    for half4 in range(2):
                        bank = 4 + half4
                        W("tensor", bankfree[bank])
                        if half4 == 0:
                            W("tensor", {"vv": st["ln1_v"]})
                        tvl = None
                        for sl in range(4):
                            dk = 4 * half4 + sl
                            def mkt(e, dk=dk, bank=bank, sl=sl):
                                return e.transpose(
                                    out=PS[bank][:, 128 * sl:128 * sl + 128],
                                    in_=x1_tm[:, 128 * dk:128 * dk + 128],
                                    identity=idf[:])
                            tvl = E("tensor", mkt, "tv" if sl == 3 else None)
                        W("scalar", {"tv": tvl})
                        W("scalar", x1fbuf_free)
                        evs = evv = None
                        for sl in range(4):
                            dk = 4 * half4 + sl
                            evs = E("scalar", lambda e, bank=bank, sl=sl, dk=dk: e.copy(
                                out=x1f_hi[:, 128 * dk:128 * dk + 128],
                                in_=PS[bank][:, 128 * sl:128 * sl + 128]), "sv")
                        W("vector", {"sv": evs})
                        W("vector", x1fbuf_free)
                        for sl in range(4):
                            dk = 4 * half4 + sl
                            evv = E("vector", lambda e, bank=bank, sl=sl, dk=dk:
                                    e.tensor_tensor(
                                        out=x1f_lo[:, 128 * dk:128 * dk + 128],
                                        in0=PS[bank][:, 128 * sl:128 * sl + 128],
                                        in1=x1f_hi[:, 128 * dk:128 * dk + 128],
                                        op=ALU.subtract), "vv")
                        bankfree[bank] = {"sv": evs, "vv": evv}
                        _merge(x1f_ready, {"sv": evs, "vv": evv})
                    st["x1f_ready"] = x1f_ready

                def mk_ffn(g):
                    def t45():
                        mms = []
                        for a, b in ES:
                            asrc = x1f_hi if a == "hi" else x1f_lo
                            wsrc = wffn_hi if b == "hi" else wffn_lo
                            for fc in range(8):
                                mms.append(lambda e, f, l2, a_=asrc, w=wsrc, fc=fc, g=g:
                                           e.matmul(out=PS[4 + g][:, 0:512],
                                                    lhsT=a_[:, 128 * fc:128 * fc + 128],
                                                    rhs=w[:, 1024 * fc + 512 * g:
                                                          1024 * fc + 512 * g + 512],
                                                    start=f, stop=l2))
                        st[f"f_tv{g}"] = mm_group(
                            4 + g, mms, extra_waits=[st["x1f_ready"], ffn_ready])
                        if g == 1:
                            x1fbuf_free.clear()
                            x1fbuf_free.update({"tv": st["f_tv1"]})
                    return t45

                def t6():
                    fv = None
                    for g in range(2):
                        W("vector", {"tv": st[f"f_tv{g}"]})
                        E("vector", lambda e, g=g: e.tensor_tensor(
                            out=x1pre[:, 512 * g:512 * g + 512], in0=PS[4 + g][:, 0:512],
                            in1=biasb[:, 512 * g:512 * g + 512], op=ALU.add), "vv")
                        fv = E("vector", lambda e, g=g: e.tensor_tensor(
                            out=x1pre[:, 512 * g:512 * g + 512],
                            in0=x1pre[:, 512 * g:512 * g + 512],
                            in1=x1_tm[:, 512 * g:512 * g + 512], op=ALU.add), "vv")
                        bankfree[4 + g] = {"vv": fv}
                    W("vector", out_free)
                    ln2_v = emit_ln(x1pre[:], out_sb[:], 1)
                    W("gpsimd", {"vv": ln2_v})
                    stv = E("gpsimd", lambda e, p_=p_, l_=l_: e.dma_start(
                        out=out_h[TP * p_ + 128 * l_:TP * p_ + 128 * l_ + 128, :],
                        in_=out_sb[:]), "st", 16)
                    out_free.clear()
                    out_free.update({"st": stv})
                    stcnt_box[0] = stv

                return [t01, t2, t3, mk_ffn(0), mk_ffn(1), t6]

            def run_tile(p, l, tail, t6_prev):
                nonlocal_state = {}
                tau = NT * p + l
                mpar = tau % 2
                W("sync", mask_free[mpar])
                ldm = E("sync", lambda e, tau=tau, mpar=mpar: e.dma_start(
                    out=mask_sb[mpar][:], in_=masks_h[tau, :, :]), f"lm{mpar}", 16)
                ldmsem = f"lm{mpar}"
                ldx_box = [None]

                s_tv = {}
                tr_tvs = {}
                av_tvs = {}
                ex_ss = {}
                ph_ss = {}
                pt_ss = {}
                ao_ss = {}
                attn_ready = {}

                def emit_S(h):
                    hp, hb = h // 2, h % 2
                    sb = hb
                    qs = 512 * hp + 128 * l
                    ks = 640 * hp + 128 * l
                    qsl = q_f[64 * hb:64 * hb + 64, qs:qs + 128]
                    gsl = gk_f[64 * hb:64 * hb + 64, 64 * hp:64 * hp + 64]
                    ksl = k_f[64 * hb:64 * hb + 64, ks:ks + 256]
                    mms = [lambda e, f, l2, sb=sb, mpar=mpar: e.matmul(
                        out=PS[sb][:, 0:NKEY], lhsT=idb[:], rhs=mask_sb[mpar][:],
                        start=True, stop=False)]
                    mms.append(lambda e, f, l2, q=qsl, g=gsl, sb=sb: e.matmul(
                        out=PS[sb][:, 0:64], lhsT=q, rhs=g, start=False, stop=False))
                    mms.append(lambda e, f, l2, q=qsl, k=ksl, sb=sb: e.matmul(
                        out=PS[sb][:, 64:NKEY], lhsT=q, rhs=k, start=False, stop=l2))
                    extra = [{ldmsem: ldm}]
                    if h < 2 and l == 0:
                        extra += [q_ready, k_ready, gkv_ready]
                    s_tv[h] = mm_group(sb, mms, extra_waits=extra)

                emit_S(0)
                emit_S(1)

                for slot in range(20):
                    for _f in range(3):
                        if fillers:
                            fillers.pop(0)()
                    if 3 <= slot <= H + 2:
                        hh_ = slot - 3
                        pr_ = hh_ % 2
                        W("vector", {"tv": tr_tvs[hh_]})
                        W("vector", ptbuf_free[pr_])
                        pt_ss[hh_] = E("vector", lambda e, pr_=pr_: e.tensor_copy(
                            out=pt_f[pr_][:], in_=PS[2 + pr_][:, 0:384]), "vv")
                        bankfree[2 + pr_] = {"vv": pt_ss[hh_]}
                    if 4 <= slot <= H + 3:
                        ho_ = slot - 4
                        hp_, hb_ = ho_ // 2, ho_ % 2
                        ob_ = 6 + ho_ % 2
                        W("scalar", {"tv": av_tvs[ho_]})
                        W("scalar", attnfree_chunk[hp_])
                        ao_ss[ho_] = E("scalar", lambda e, hb=hb_, hp=hp_, ob=ob_: e.copy(
                            out=attn_hi[64 * hb:64 * hb + 64, 128 * hp:128 * hp + 128],
                            in_=PS[ob][0:64, 0:128]), "sv")
                    if slot == 2:
                        if t6_prev:
                            t6_prev[0]()
                        if tail:
                            tail[0]()
                    if tail and slot == 7:
                        tail[1]()
                    if slot == 9:
                        W("sync", xres_free)
                        ldx_box[0] = E("sync", lambda e, p=p, l=l: e.dma_start(
                            out=xres[:],
                            in_=x_h[TP * p + 128 * l:TP * p + 128 * l + 128, :]),
                            "lr", 16)
                    if tail:
                        if slot == 11:
                            tail[2]()
                        elif slot == 13:
                            tail[3]()
                        elif slot == 14:
                            tail[4]()
                    if slot < H:
                        h = slot
                        sb = h % 2
                        W("vector", {"tv": s_tv[h]})
                        nm_v = E("vector", lambda e, sb=sb, h=h: e.reduce_max(
                            out=negm[:, h:h + 1], in_=PS[sb][:, 0:NKEY], axis=AX.X,
                            negate=True), "vv")
                        W("scalar", {"vv": nm_v})
                        W("scalar", ehb_free[h % 2]) if False else None
                        W("scalar", ehb_free[h % 2])
                        ex_s = E("scalar", lambda e, sb=sb, h=h: e.activation(
                            out=e_sb[h % 2][:], in_=PS[sb][:, 0:NKEY], func=AF.Exp,
                            bias=negm[:, h:h + 1], scale=1.0,
                            accum_out=sumv[:, h:h + 1]), "sv")
                        ex_ss[h] = ex_s
                        bankfree[sb] = {"sv": ex_s, "vv": nm_v}
                    if 1 <= slot <= H:
                        h = slot - 1
                        W("scalar", {"sv": ex_ss[h]})
                        a_ln = E("scalar", lambda e, h=h: e.activation(
                            out=recv[:, h:h + 1], in_=sumv[:, h:h + 1], func=AF.Ln,
                            bias=0.0, scale=1.0), "sv")
                        W("scalar", {"sv": a_ln})
                        a_rs = E("scalar", lambda e, h=h: e.activation(
                            out=sumv[:, h:h + 1], in_=recv[:, h:h + 1], func=AF.Exp,
                            bias=0.0, scale=-1.0), "sv")
                        W("scalar", {"sv": a_rs})
                        W("scalar", phl_free[h % 2])
                        ph_s = E("scalar", lambda e, h=h: e.activation(
                            out=p_f[h % 2][:], in_=e_sb[h % 2][:], func=AF.Identity,
                            bias=0.0, scale=sumv[:, h:h + 1]), "sv")
                        ehb_free[h % 2] = {"sv": ph_s}
                        ph_ss[h] = ph_s
                    if 2 <= slot <= H + 1:
                        h = slot - 2
                        ptb = 2 + h % 2
                        W("tensor", bankfree[ptb])
                        W("tensor", {"sv": ph_ss[h]})
                        tr_tv = None
                        for ci, c0 in enumerate((0, 64, 192)):
                            def mkt(e, c0=c0, ci=ci, ptb=ptb, h=h):
                                return e.transpose(
                                    out=PS[ptb][:, 128 * ci:128 * ci + 128],
                                    in_=p_f[h % 2][:, c0:c0 + 128], identity=idf[:])
                            tr_tv = E("tensor", mkt, "tv" if ci == 2 else None)
                        tr_tvs[h] = tr_tv
                        phl_free[h % 2] = {"tv": tr_tv}
                    if 3 <= slot <= H + 2:
                        h = slot - 3
                        pr = h % 2
                        pt_s = pt_ss[h]
                        ob = 6 + pr
                        mms = []
                        for ci in range(3):
                            if ci == 0:
                                vsl = gv_f[0:64, 64 * h:64 * h + 64]
                                prt = 64
                            else:
                                j = l + ci - 1
                                vsl = v_f[:, 1024 * j + 64 * h:1024 * j + 64 * h + 64]
                                prt = 128
                            mms.append(lambda e, f, l2, v=vsl, pr=pr, ci=ci, prt=prt,
                                       ob=ob: e.matmul(
                                           out=PS[ob][0:64, 0:128], lhsT=v,
                                           rhs=pt_f[pr][0:prt, 128 * ci:128 * ci + 128],
                                           start=f, stop=l2))
                        av_extra = [{"vv": pt_s}]
                        if h == 0 and l == 0:
                            av_extra.append(v_ready)
                        av_tvs[h] = mm_group(ob, mms, extra_waits=av_extra)
                        ptbuf_free[pr] = {"tv": av_tvs[h]}
                    if 4 <= slot <= H + 3:
                        h = slot - 4
                        hp2, hb2 = h // 2, h % 2
                        ob = 6 + h % 2
                        ao_s = ao_ss[h]
                        W("vector", {"sv": ao_s})
                        W("vector", attnfree_chunk[hp2])
                        ao_v = E("vector", lambda e, hb=hb2, hp=hp2, ob=ob: e.tensor_tensor(
                            out=attn_lo[64 * hb:64 * hb + 64, 128 * hp:128 * hp + 128],
                            in0=PS[ob][0:64, 0:128],
                            in1=attn_hi[64 * hb:64 * hb + 64, 128 * hp:128 * hp + 128],
                            op=ALU.subtract), "vv")
                        bankfree[ob] = {"sv": ao_s, "vv": ao_v}
                        _merge(attn_ready, {"sv": ao_s, "vv": ao_v})
                    if slot + 2 < H:
                        emit_S(slot + 2)

                mask_free[mpar] = {"tv": s_tv[H - 1]}
                passend["attn_tv"] = max(s_tv[H - 1], av_tvs[H - 1])
                return make_tail(p, l, ldx_box[0], attn_ready)

            for l in range(NT):
                new_tail = run_tile(p, l, pending_tail[0], pending_t6[0])
                pending_t6[0] = [pending_tail[0][5]] if pending_tail[0] else None
                pending_tail[0] = new_tail
            assert not fillers, "weight-prep fillers must drain within the pass"
            if p == 0:
                _merge(xfm_free, ffn_ready)

            kvbuf_free = {"tv": passend["attn_tv"]}
            xfm_free = {"tv": qkv_last_tv}
            xpf_free = dict(xfm_free)


        if pending_t6[0] is not None:
            for stage in pending_t6[0]:
                stage()
            pending_t6[0] = None
        if pending_tail[0] is not None:
            for stage in pending_tail[0]:
                stage()
            pending_tail[0] = None
        W("gpsimd", {"st": stcnt_box[0]})

        @block.sync
        def _(eng):
            for f in Q["sync"]:
                f(eng)

        @block.gpsimd
        def _(eng):
            for f in Q["gpsimd"]:
                f(eng)

        @block.tensor
        def _(eng):
            for f in Q["tensor"]:
                f(eng)

        @block.vector
        def _(eng):
            for f in Q["vector"]:
                f(eng)

        @block.scalar
        def _(eng):
            for f in Q["scalar"]:
                f(eng)

    return nc


def _build_masks(half):
    r = np.arange(128)[:, None]
    c = np.arange(256)[None, :]
    win_generic = (r <= c) & (c <= r + 128)
    m = np.full((16, 128, NKEY), MASKV, np.float32)
    for t in range(16):
        m[t, :, 0:64] = 0.0
        m[t, :, 64:NKEY][win_generic] = 0.0
    if half == 0:
        m[0, :, 0:64] = MASKV
        m[0, :, 64:NKEY] = MASKV
        m[0, :, 64:NKEY][(c >= 128) & ((c - 128) <= r)] = 0.0
        m[1, :, 0:64] = MASKV
        m[1, :, 64:NKEY] = MASKV
        m[1, :, 64:NKEY][(c <= 128 + r) & ((c < 64) | (c >= r))] = 0.0
    return np.ascontiguousarray(m.astype(BF))


def _host_inputs(core, L0_attn_x, L0_attn_w_qkv, L0_attn_w_out,
                 L0_ffn_G1_R, L0_ffn_G2_R, L0_ffn_G1_I, L0_ffn_G2_I,
                 L0_ffn_bias_real, L0_ffn_bias_imag):
    b, half = core // 2, core % 2
    X = np.asarray(L0_attn_x, np.float32)
    x = np.ascontiguousarray(X[b, RB * half:RB * (half + 1)])
    xpre = np.zeros((192, D), np.float32)
    xpre[0:64] = X[b, 0:64]
    if half == 1:
        xpre[64:192] = X[b, RB - 128:RB]

    def tt_bcast(g1, g2):
        g1 = np.asarray(g1, np.float32)
        g2 = np.asarray(g2, np.float32)
        t = np.transpose(g1[0], (1, 2, 0))
        U = np.broadcast_to(t[:, :, None, :, None],
                            (4, 32, 16, 32, 16)).reshape(4, 512, 512)
        u = np.transpose(g2[:, :, 0, :], (0, 2, 1))
        V = np.broadcast_to(u[:, None, :, None, :],
                            (4, 32, 16, 32, 16)).reshape(4, 512, 512)
        return U, V

    UR, VR = tt_bcast(L0_ffn_G1_R, L0_ffn_G2_R)
    UI, VI = tt_bcast(L0_ffn_G1_I, L0_ffn_G2_I)
    ttu = np.ascontiguousarray(np.concatenate([UR, UI], 0))
    ttv = np.ascontiguousarray(np.concatenate([VR, VI], 0))
    bias_cat = np.concatenate([np.asarray(L0_ffn_bias_real, np.float32),
                               np.asarray(L0_ffn_bias_imag, np.float32)])
    biasb = np.ascontiguousarray(np.broadcast_to(bias_cat[None, :], (128, D)))
    return dict(
        x=x, xpre=xpre,
        wqkv=np.ascontiguousarray(np.asarray(L0_attn_w_qkv, np.float32)),
        wout=np.ascontiguousarray(np.asarray(L0_attn_w_out, np.float32)),
        ttu=ttu, ttv=ttv, biasb=biasb,
        masks=_build_masks(half),
        idf=np.eye(128, dtype=np.float32),
        idb=np.eye(128, dtype=BF),
    )


def kernel(**inputs):
    nc = build_program()
    in_maps = [_host_inputs(c, **inputs) for c in range(NCORE)]
    res = run_bass_kernel_spmd(nc, in_maps, core_ids=list(range(NCORE)))
    out = np.empty((B, S, D), np.float32)
    for c in range(NCORE):
        b, half = c // 2, c % 2
        out[b, RB * half:RB * (half + 1)] = res.results[c]["out"]
    return out
